# version 02
# baseline (speedup 1.0000x reference)
import numpy as np
import ml_dtypes
import concourse.bass as bass
import concourse.mybir as mybir
from concourse.bass_utils import run_bass_kernel_spmd

F32 = mybir.dt.float32
BF16 = mybir.dt.bfloat16
AF = mybir.ActivationFunctionType
ALU = mybir.AluOpType
AX = mybir.AxisListType

NCORES = 8
NB = 4
S = 2048
D = 1024
TT = 16
KT = 8
NEXP = 8
ATTN_STAGES = {"kv", "qt", "scores", "proj"}
EPS1 = 1e-5
EPS2 = 1e-6


ENGS = ("PE", "ACT", "DVE", "POOL", "SP")
_BLOCK_ATTR = {"PE": "tensor", "ACT": "scalar", "DVE": "vector", "POOL": "gpsimd", "SP": "sync"}
N_LANES = 4
ELIDE_SAME_ENGINE = ("PE",)


class Sched:
    def __init__(self):
        self.ops = []
        self.tok = {}
        self.lane_rr = {}
        self.lane_cnt = {}
        self.lane_last = {}

    def _t(self, t):
        d = self.tok.get(t)
        if d is None:
            d = {"w": None, "r": {}, "pend": None}
            self.tok[t] = d
        return d

    EXCLUSIVE = frozenset(["P0a", "P0b", "P1a", "P1b", "P2a", "P2b", "P3", "PT"])

    def _record(self, eng, kind, reads, writes, emit, queue=None):
        i = len(self.ops)
        excl = [t for t in reads if t in self.EXCLUSIVE]
        if excl:
            reads = [t for t in reads if t not in self.EXCLUSIVE]
            writes = list(writes) + excl
        deps = set()
        for t in reads:
            d = self._t(t)
            if d["pend"]:
                deps |= d["pend"]
            if d["w"] is not None:
                deps.add(d["w"])
        for t in writes:
            d = self._t(t)
            if d["pend"]:
                deps |= d["pend"]
            if d["w"] is not None:
                deps.add(d["w"])
            deps |= set(d["r"].values())
        op = {
            "i": i, "eng": eng, "kind": kind, "deps": deps, "emit": emit,
            "queue": queue, "lane": None, "lcnt": None, "inc": False, "inc_id": None,
        }
        self.ops.append(op)
        if kind == "d":
            q = queue
            lane = self.lane_rr.get(q, 0)
            self.lane_rr[q] = (lane + 1) % N_LANES
            cnt = self.lane_cnt.get((q, lane), 0) + 1
            self.lane_cnt[(q, lane)] = cnt
            op["lane"] = lane
            op["lcnt"] = cnt
            prev = self.lane_last.get((q, lane))
            if prev is not None:
                deps.add(prev)
            self.lane_last[(q, lane)] = i
        rk = ("c", eng) if kind == "c" else ("d", queue, op["lane"])
        for t in reads:
            self._t(t)["r"][rk] = i
        for t in writes:
            self.tok[t] = {"w": i, "r": {}, "pend": None}
        deps.discard(i)
        return i

    def op(self, eng, reads, writes, emit):
        return self._record(eng, "c", reads, writes, emit)

    def dma(self, queue, reads, writes, emit):
        return self._record(queue, "d", reads, writes, emit, queue=queue)

    def alias(self, dying, new):
        deps = set()
        for t in dying:
            d = self.tok.get(t)
            if d is None:
                continue
            if d["w"] is not None:
                deps.add(d["w"])
            deps |= set(d["r"].values())
            if d["pend"]:
                deps |= d["pend"]
            del self.tok[t]
        for t in new:
            d = self.tok.get(t)
            old = set(deps)
            if d is not None:
                if d["w"] is not None:
                    old.add(d["w"])
                old |= set(d["r"].values())
                if d["pend"]:
                    old |= d["pend"]
            self.tok[t] = {"w": None, "r": {}, "pend": old}

    def finalize(self, nc):
        ops = self.ops
        for op in ops:
            pruned = set()
            for d in op["deps"]:
                p = ops[d]
                if (p["kind"] == "c" and op["kind"] == "c" and p["eng"] == op["eng"]
                        and p["eng"] in ELIDE_SAME_ENGINE):
                    continue
                pruned.add(d)
            op["deps"] = pruned
            for d in pruned:
                p = ops[d]
                if p["kind"] == "c":
                    p["inc"] = True
        last_per = {}
        for op in ops:
            if op["kind"] == "c":
                last_per[("c", op["eng"])] = op["i"]
            else:
                last_per[("d", op["queue"], op["lane"])] = op["i"]
        drain_deps = set(last_per.values())
        drain = {"i": len(ops), "eng": "SP", "kind": "c", "deps": drain_deps,
                 "emit": None, "queue": None, "lane": None, "lcnt": None,
                 "inc": False, "inc_id": None}
        ops.append(drain)
        for d in drain_deps:
            if ops[d]["kind"] == "c":
                ops[d]["inc"] = True
        cnt = {e: 0 for e in ENGS}
        for op in ops:
            if op["kind"] == "c" and op["inc"]:
                cnt[op["eng"]] += 1
                op["inc_id"] = cnt[op["eng"]]
        sem_eng = {e: nc.alloc_semaphore(name=f"se_{e}") for e in ENGS}
        sem_lane = {}
        for (q, lane) in self.lane_cnt:
            sem_lane[(q, lane)] = nc.alloc_semaphore(name=f"sd_{q}_{lane}")
        clock = {e: {} for e in ENGS}
        snap = {}
        for op in ops:
            ek = op["eng"] if op["kind"] == "c" else op["queue"]
            c = clock[ek]
            need = {}
            prod = {}
            for d in op["deps"]:
                p = ops[d]
                if p["kind"] == "c":
                    if (op["kind"] == "c" and p["eng"] == op["eng"]
                            and p["eng"] in ELIDE_SAME_ENGINE):
                        continue
                    sk = ("e", p["eng"])
                    v = p["inc_id"]
                else:
                    sk = ("l", p["queue"], p["lane"])
                    v = 16 * p["lcnt"]
                if v is None:
                    continue
                if need.get(sk, -1) < v:
                    need[sk] = v
                    prod[sk] = d
            waits = []
            for sk, v in sorted(need.items(), key=lambda kv: str(kv[0])):
                if c.get(sk, 0) >= v:
                    continue
                waits.append((sk, v))
                ps = snap.get(prod[sk])
                if ps:
                    for k2, v2 in ps.items():
                        if c.get(k2, 0) < v2:
                            c[k2] = v2
                if c.get(sk, 0) < v:
                    c[sk] = v
            op["waits"] = waits
            s = dict(c)
            if op["kind"] == "c":
                if op["inc"]:
                    s[("e", op["eng"])] = op["inc_id"]
            else:
                s[("l", op["queue"], op["lane"])] = 16 * op["lcnt"]
            snap[op["i"]] = s
        self.sem_eng = sem_eng
        self.sem_lane = sem_lane

    def emit(self, nc):
        per_eng = {e: [] for e in ENGS}
        for op in self.ops:
            ek = op["eng"] if op["kind"] == "c" else op["queue"]
            per_eng[ek].append(op)
        sem_eng, sem_lane = self.sem_eng, self.sem_lane

        def run(eng_obj, ops_list):
            for op in ops_list:
                for sk, v in op["waits"]:
                    sem = sem_eng[sk[1]] if sk[0] == "e" else sem_lane[(sk[1], sk[2])]
                    eng_obj.wait_ge(sem, v)
                if op["emit"] is None:
                    continue
                inst = op["emit"](eng_obj)
                if op["kind"] == "d":
                    inst.then_inc(sem_lane[(op["queue"], op["lane"])], 16)
                elif op["inc"]:
                    inst.then_inc(sem_eng[op["eng"]], 1)

        with nc.Block() as block:
            for e in ENGS:
                if not per_eng[e]:
                    continue
                getattr(block, _BLOCK_ATTR[e])(
                    (lambda lst: (lambda eng_obj: run(eng_obj, lst)))(per_eng[e])
                )



SB = {}


def _place(name, off, shape, dt):
    off = (off + 31) & ~31
    SB[name] = (off, shape, dt)
    return off + shape[1] * (4 if dt == F32 else 2)


def build_mem_map():
    SB.clear()
    o = 0
    o = _place("x", o, [128, TT * 1024], F32)
    o = _place("gam", o, [128, 1024], F32)
    o = _place("bet", o, [128, 1024], F32)
    o = _place("bpj", o, [128, 1024], F32)
    o = _place("rbias", o, [128, 8], F32)
    o = _place("rwT", o, [128, 64], BF16)
    o = _place("ident", o, [128, 128], BF16)
    o = _place("onesb", o, [128, 64], BF16)
    o = _place("chalf", o, [128, 1], F32)
    o = _place("ceps1", o, [128, 1], F32)
    o = _place("ceps2", o, [128, 1], F32)
    o = _place("stats", o, [128, 64], F32)
    o = _place("gsc", o, [128, 128], F32)
    o = _place("s_sb", o, [128, 128], F32)
    o = _place("comb05", o, [128, 128], F32)
    o = _place("rfull", o, [128, 512], BF16)
    o = (o + 31) & ~31
    base = o
    _place("xnT", base, [128, 16384], BF16)
    for sl in (0, 1):
        w0 = base + 32768 + sl * 24576
        w0 = _place(f"sw1c{sl}", w0, [128, 4096], BF16)
        w0 = _place(f"sw3c{sl}", w0, [128, 4096], BF16)
        w0 = _place(f"sw2c{sl}", w0, [128, 4096], BF16)
    _place("gT", base + 81920, [128, 8192], BF16)
    _place("sel", base + 81920, [128, 128], F32)
    _place("m1", base + 82432, [128, 128], F32)
    _place("m2", base + 82944, [128, 128], F32)
    _place("siluT0", base + 98304, [128, 1024], BF16)
    _place("siluT1", base + 100352, [128, 1024], BF16)
    _place("xnm0", base + 102400, [128, 1024], BF16)
    _place("xnm1", base + 104448, [128, 1024], BF16)
    _place("ew1", base + 32768, [128, 8192], BF16)
    _place("ew2_0", base + 49152, [128, 8192], BF16)
    _place("ew2_1", base + 65536, [128, 8192], BF16)
    _place("hT0", base + 81920, [128, 8192], BF16)
    _place("hT1", base + 98304, [128, 8192], BF16)
    _place("KTb", base + 0, [128, 16384], BF16)
    _place("V", base + 32768, [128, 16640], BF16)
    _place("wkv", base + 66048, [128, 8192], BF16)
    _place("wpj", base + 82432, [128, 8192], BF16)
    _place("xnTq", base + 98816, [128, 4096], BF16)
    _place("QTq", base + 107008, [128, 4096], BF16)
    _place("OTq", base + 115200, [128, 4096], BF16)
    _place("Ebuf", base + 123392, [128, 2048], BF16)
    _place("Bsb", base + 127488, [128, 512], BF16)
    _place("xnma", base + 128512, [128, 1024], BF16)
    return base + 130560


SHARED_TOKS = (["sel_t", "m1_t", "m2_t"]
               + [f"sw{w}c{s}" for w in (1, 3, 2) for s in (0, 1)]
               + [f"gT{m}_{c}" for m in range(4) for c in range(4)]
               + ["siluT0", "siluT1", "xnm0", "xnm1"])
ROUTED_TOKS = (["ew1", "ew2_0", "ew2_1"]
               + [f"hT0_{m}" for m in range(8)] + [f"hT1_{m}" for m in range(8)])
FFN_TOKS = [f"xnT{t}" for t in range(TT)] + SHARED_TOKS + ROUTED_TOKS
ATT_TOKS = ([f"KTc{c}" for c in range(4)] + [f"Vt{t}" for t in range(TT)]
            + ["Vones", "wkv", "wpj", "xnTq"] + [f"QT{m}" for m in range(8)]
            + [f"OT{h}" for h in range(16)] + ["E0", "E1", "Bsb", "xnma"])



def build_nc(nb=NB, sublayers=("f", "a", "f")):
    nc = bass.Bass("TRN2", target_bir_lowering=False)
    build_mem_map()

    dr = {}
    dr["xin"] = nc.dram_tensor("xin", [128, nb * TT * 1024], F32, kind="ExternalInput")
    dr["y"] = nc.dram_tensor("y", [128, nb * TT * 1024], F32, kind="ExternalOutput")
    for n, f in [("wq", 8192), ("wk", 8192), ("wv", 8192), ("wproj", 8192),
                 ("sw1", 16384), ("sw3", 16384), ("sw2", 16384),
                 ("ew1", 8 * 8192), ("ew2", 8 * 8192)]:
        dr[n] = nc.dram_tensor(n, [128, f], F32, kind="ExternalInput")
    dr["bpj"] = nc.dram_tensor("bpj", [128, 1024], F32, kind="ExternalInput")
    dr["gam"] = nc.dram_tensor("gam", [128, 1024], F32, kind="ExternalInput")
    dr["bet"] = nc.dram_tensor("bet", [128, 1024], F32, kind="ExternalInput")
    dr["rw"] = nc.dram_tensor("rw", [128, 64], F32, kind="ExternalInput")
    dr["rb"] = nc.dram_tensor("rb", [128, 8], F32, kind="ExternalInput")
    dr["identc"] = nc.inline_tensor(np.eye(128, dtype=ml_dtypes.bfloat16), name="identc")

    base0 = (229376 - nc.sbuf_bytes_remaining + 31) & ~31
    sb = {}
    for name, (off, shape, dt) in SB.items():
        sb[name] = nc.alloc_sbuf_tensor_at(name, shape, dt, offset=base0 + off)

    psums = (
        nc.alloc_psum_tensor("P0", [128, 1024], F32),
        nc.alloc_psum_tensor("P1", [128, 1024], F32),
        nc.alloc_psum_tensor("P2", [128, 1024], F32),
        nc.alloc_psum_tensor("P3", [128, 512], F32),
        nc.alloc_psum_tensor("PTp", [128, 1024], BF16),
    )

    P = Sched()
    build_program(P, dr, sb, psums, nb=nb, sublayers=sublayers)
    P.finalize(nc)
    with nc.allow_low_precision(reason="bf16 matmul operand stores"):
        P.emit(nc)
    return nc


def build_program(P, dr, sb, psums, nb=NB, sublayers=("f", "a", "f")):
    x = sb["x"]
    P.dma("SP", [], ["gam"], lambda e: e.dma_start(sb["gam"][:, :], dr["gam"][:, :]))
    P.dma("SP", [], ["bet"], lambda e: e.dma_start(sb["bet"][:, :], dr["bet"][:, :]))
    P.dma("SP", [], ["bpj"], lambda e: e.dma_start(sb["bpj"][:, :], dr["bpj"][:, :]))
    P.dma("SP", [], ["rbias"], lambda e: e.dma_start(sb["rbias"][:, :], dr["rb"][:, :]))
    P.dma("SP", [], ["ident"], lambda e: e.dma_start(sb["ident"][:, :], dr["identc"].ap()))
    P.dma("POOL", [], ["rwT"], lambda e: e.dma_start(sb["rwT"][:, :], dr["rw"][:, :]))
    P.op("DVE", [], ["onesb"], lambda e: e.memset(sb["onesb"][:, :], 1.0))
    P.op("DVE", [], ["chalf"], lambda e: e.memset(sb["chalf"][:, :], 0.5))
    P.op("DVE", [], ["ceps1"], lambda e: e.memset(sb["ceps1"][:, :], EPS1))
    P.op("DVE", [], ["ceps2"], lambda e: e.memset(sb["ceps2"][:, :], EPS2))

    prev = None
    for b in range(nb):
        xoff = b * TT * 1024
        P.dma("SP", [f"x{t}" for t in range(TT)], [f"x{t}" for t in range(TT)],
              (lambda xo=xoff: lambda e: e.dma_start(x[:, :], dr["xin"][:, xo:xo + TT * 1024]))())
        for kind in sublayers:
            if kind == "f":
                dying = ATT_TOKS if prev == "a" else (ROUTED_TOKS if prev == "f" else [])
                P.alias(dying, FFN_TOKS)
                ffn_sublayer(P, dr, sb, psums)
            else:
                P.alias(FFN_TOKS if prev == "f" else [], ATT_TOKS)
                attn_sublayer(P, dr, sb, psums)
            prev = kind
        P.dma("SP", [f"x{t}" for t in range(TT)], [],
              (lambda xo=xoff: lambda e: e.dma_start(dr["y"][:, xo:xo + TT * 1024], x[:, :]))())



def ln_stats(P, sb, eps_tok, scraps):
    x, st = sb["x"], sb["stats"]
    for t in range(TT):
        xt = x[:, t * 1024:(t + 1) * 1024]
        P.op("DVE", [f"x{t}"], [f"st_s{t}"],
             (lambda xt=xt, t=t: lambda e: e.reduce_sum(st[:, t:t + 1], xt, axis=AX.X))())
        sc = scraps[t % len(scraps)]
        P.op("ACT", [f"x{t}"], [f"st_q{t}", sc],
             (lambda xt=xt, t=t, sc=sc: lambda e: e.activation(
                 sb[sc][:, :], xt, AF.Square, accum_out=st[:, 16 + t:17 + t]))())
    rs = [f"st_s{t}" for t in range(TT)] + [f"st_q{t}" for t in range(TT)]
    P.op("DVE", rs, ["st_mean"],
         lambda e: e.tensor_scalar_mul(st[:, 0:16], st[:, 0:16], 1.0 / 1024.0))
    P.op("DVE", rs, ["st_ex2"],
         lambda e: e.tensor_scalar_mul(st[:, 16:32], st[:, 16:32], 1.0 / 1024.0))
    P.op("DVE", ["st_mean"], ["st_msq"],
         lambda e: e.tensor_mul(st[:, 48:64], st[:, 0:16], st[:, 0:16]))
    P.op("DVE", ["st_ex2", "st_msq"], ["st_var"],
         lambda e: e.tensor_sub(st[:, 16:32], st[:, 16:32], st[:, 48:64]))
    P.op("ACT", ["st_var", eps_tok], ["st_std"],
         (lambda et=eps_tok: lambda e: e.activation(
             st[:, 32:48], st[:, 16:32], AF.Sqrt, bias=sb[et][:, :]))())
    P.op("DVE", ["st_std"], ["st_rstd"],
         lambda e: e.reciprocal(st[:, 32:48], st[:, 32:48]))
    P.op("DVE", ["st_rstd", "st_mean"], ["st_mrs"],
         lambda e: e.tensor_mul(st[:, 48:64], st[:, 0:16], st[:, 32:48]))


def xn_tile(P, sb, t, out_tok, out_ap):
    x, st = sb["x"], sb["stats"]
    P.op("DVE", [f"x{t}", "st_rstd", "st_mrs"], [out_tok],
         (lambda t=t, out_ap=out_ap: lambda e: e.tensor_scalar(
             out_ap, x[:, t * 1024:(t + 1) * 1024],
             st[:, 32 + t:33 + t], st[:, 48 + t:49 + t], ALU.mult, ALU.subtract))())


def transpose_tile(P, sb, psums, src_tok, src_sb, dst_tok, dst_ap_fn):
    PTp = psums[4]
    for k in range(KT):
        P.op("PE", [src_tok, "ident"], ["PT"],
             (lambda k=k, src_sb=src_sb: lambda e: e.transpose(
                 PTp[:, k * 128:(k + 1) * 128], src_sb[:, k * 128:(k + 1) * 128],
                 sb["ident"][:, :]))())
    P.op("DVE", ["PT"], [dst_tok],
         (lambda f=dst_ap_fn: lambda e: e.tensor_copy(
             f(), PTp[:, :].rearrange("p (k c) -> p k c", c=128)))())



def ffn_sublayer(P, dr, sb, psums):
    P0, P1, P2, P3, PTp = psums
    x, st, xnT = sb["x"], sb["stats"], sb["xnT"]

    ln_stats(P, sb, "ceps1", ("xnm0", "xnm1"))

    for t in range(TT):
        slot = "xnm0" if t % 2 == 0 else "xnm1"
        xn_tile(P, sb, t, slot, sb[slot][:, :])
        dst = (lambda t=t: xnT[:, :].rearrange("p (k s) -> p k s", s=2048)[:, :, t * 128:(t + 1) * 128])
        transpose_tile(P, sb, psums, slot, sb[slot], f"xnT{t}", dst)

    s_sb, sel, m1, m2, gsc, comb = sb["s_sb"], sb["sel"], sb["m1"], sb["m2"], sb["gsc"], sb["comb05"]
    for t in range(TT):
        for k in range(KT):
            P.op("PE", [f"xnT{t}", "rwT"], ["P3"],
                 (lambda t=t, k=k: lambda e: e.matmul(
                     P3[:, 0:8], xnT[:, k * 2048 + t * 128: k * 2048 + (t + 1) * 128],
                     sb["rwT"][:, k * 8:(k + 1) * 8], start=(k == 0), stop=(k == KT - 1)))())
        P.op("ACT", ["P3"], [f"s{t}"],
             (lambda t=t: lambda e: e.activation(s_sb[:, t * 8:(t + 1) * 8], P3[:, 0:8], AF.Sigmoid))())
    sread = [f"s{t}" for t in range(TT)]
    s3 = s_sb[:, :].rearrange("p (t e) -> p t e", e=8)
    sel3 = sel[:, :].rearrange("p (t e) -> p t e", e=8)
    m13 = m1[:, :].rearrange("p (t e) -> p t e", e=8)
    m23 = m2[:, :].rearrange("p (t e) -> p t e", e=8)
    c3 = comb[:, :].rearrange("p (t e) -> p t e", e=8)
    rb3 = sb["rbias"][:, :].unsqueeze(1).broadcast_to([128, TT, 8])
    P.op("DVE", sread + ["rbias"], ["sel_t"], lambda e: e.tensor_tensor(sel3, s3, rb3, ALU.add))
    P.op("DVE", ["sel_t"], ["g_m1"], lambda e: e.reduce_max(gsc[:, 0:16], sel3, axis=AX.X))
    m1b = gsc[:, 0:16].unsqueeze(2).broadcast_to([128, TT, 8])
    P.op("DVE", ["sel_t", "g_m1"], ["m1_t"], lambda e: e.tensor_tensor(m13, sel3, m1b, ALU.is_equal))
    P.op("DVE", ["sel_t", "m1_t"], ["sel_t"],
         lambda e: e.scalar_tensor_tensor(sel3, m13, -1e9, sel3, ALU.mult, ALU.add))
    P.op("DVE", ["sel_t"], ["g_m2"], lambda e: e.reduce_max(gsc[:, 16:32], sel3, axis=AX.X))
    m2b = gsc[:, 16:32].unsqueeze(2).broadcast_to([128, TT, 8])
    P.op("DVE", ["sel_t", "g_m2"], ["m2_t"], lambda e: e.tensor_tensor(m23, sel3, m2b, ALU.is_equal))
    P.op("DVE", ["m1_t"] + sread, ["sel_t"], lambda e: e.tensor_tensor(sel3, m13, s3, ALU.mult))
    P.op("DVE", ["sel_t"], ["g_s1"], lambda e: e.reduce_sum(gsc[:, 32:48], sel3, axis=AX.X))
    P.op("DVE", ["m2_t"] + sread, ["sel_t"], lambda e: e.tensor_tensor(sel3, m23, s3, ALU.mult))
    P.op("DVE", ["sel_t"], ["g_s2"], lambda e: e.reduce_sum(gsc[:, 48:64], sel3, axis=AX.X))
    P.op("DVE", ["g_s1", "g_s2"], ["g_den"],
         lambda e: e.tensor_add(gsc[:, 64:80], gsc[:, 32:48], gsc[:, 48:64]))
    P.op("DVE", ["g_den"], ["g_dn9"],
         lambda e: e.tensor_scalar_add(gsc[:, 80:96], gsc[:, 64:80], 1e-9))
    P.op("DVE", ["g_dn9"], ["g_inv"], lambda e: e.reciprocal(gsc[:, 80:96], gsc[:, 80:96]))
    P.op("DVE", ["g_s1", "g_inv"], ["g_s1"],
         lambda e: e.tensor_mul(gsc[:, 32:48], gsc[:, 32:48], gsc[:, 80:96]))
    P.op("DVE", ["g_s2", "g_inv"], ["g_s2"],
         lambda e: e.tensor_mul(gsc[:, 48:64], gsc[:, 48:64], gsc[:, 80:96]))
    P.op("DVE", ["g_den"], ["g_cond"],
         lambda e: e.tensor_single_scalar(gsc[:, 96:112], gsc[:, 64:80], 1e-9, ALU.is_gt))
    for cols, tok in (((32, 48), "g_s1"), ((48, 64), "g_s2")):
        a, b = cols
        P.op("DVE", [tok], [tok],
             (lambda a=a, b=b: lambda e: e.tensor_scalar_sub(gsc[:, a:b], gsc[:, a:b], 0.5))())
        P.op("DVE", [tok, "g_cond"], [tok],
             (lambda a=a, b=b: lambda e: e.tensor_tensor(
                 gsc[:, a:b], gsc[:, a:b], gsc[:, 96:112], ALU.mult))())
        P.op("DVE", [tok], [tok],
             (lambda a=a, b=b: lambda e: e.tensor_scalar_add(gsc[:, a:b], gsc[:, a:b], 0.5))())
    g1b = gsc[:, 32:48].unsqueeze(2).broadcast_to([128, TT, 8])
    g2b = gsc[:, 48:64].unsqueeze(2).broadcast_to([128, TT, 8])
    P.op("DVE", ["m1_t", "g_s1"], ["m1_t"], lambda e: e.tensor_tensor(m13, m13, g1b, ALU.mult))
    P.op("DVE", ["m2_t", "g_s2"], ["m2_t"], lambda e: e.tensor_tensor(m23, m23, g2b, ALU.mult))
    P.op("DVE", ["m1_t", "m2_t"], ["comb_t"], lambda e: e.tensor_tensor(c3, m13, m23, ALU.add))
    P.op("DVE", ["comb_t"], ["comb05"],
         lambda e: e.tensor_scalar_mul(comb[:, :], comb[:, :], 0.5))

    pool = [("P0a", P0, 0), ("P0b", P0, 512), ("P1a", P1, 0),
            ("P1b", P1, 512), ("P2a", P2, 0), ("P2b", P2, 512)]
    for c in range(4):
        sl = c % 2
        w1, w3, w2 = sb[f"sw1c{sl}"], sb[f"sw3c{sl}"], sb[f"sw2c{sl}"]
        P.dma("POOL", [], [f"sw1c{sl}"],
              (lambda c=c, w1=w1: lambda e: e.dma_start(w1[:, :], dr["sw1"][:, c * 4096:(c + 1) * 4096]))())
        P.dma("POOL", [], [f"sw3c{sl}"],
              (lambda c=c, w3=w3: lambda e: e.dma_start(w3[:, :], dr["sw3"][:, c * 4096:(c + 1) * 4096]))())
        P.dma("POOL", [], [f"sw2c{sl}"],
              (lambda c=c, w2=w2: lambda e: e.dma_start(w2[:, :], dr["sw2"][:, c * 4096:(c + 1) * 4096]))())
        it = 0
        for mh in range(4):
            for tc in range(4):
                pa_t, pa, poa = pool[(2 * it) % 6]
                pb_t, pb, pob = pool[(2 * it + 1) % 6]
                xtoks = [f"xnT{4 * tc + j}" for j in range(4)]
                for k in range(KT):
                    P.op("PE", xtoks + [f"sw1c{sl}"], [pa_t],
                         (lambda k=k, mh=mh, tc=tc, w1=w1, pa=pa, poa=poa: lambda e: e.matmul(
                             pa[:, poa:poa + 512],
                             w1[:, k * 512 + mh * 128: k * 512 + (mh + 1) * 128],
                             sb["xnT"][:, k * 2048 + tc * 512: k * 2048 + (tc + 1) * 512],
                             start=(k == 0), stop=(k == KT - 1)))())
                for k in range(KT):
                    P.op("PE", xtoks + [f"sw3c{sl}"], [pb_t],
                         (lambda k=k, mh=mh, tc=tc, w3=w3, pb=pb, pob=pob: lambda e: e.matmul(
                             pb[:, pob:pob + 512],
                             w3[:, k * 512 + mh * 128: k * 512 + (mh + 1) * 128],
                             sb["xnT"][:, k * 2048 + tc * 512: k * 2048 + (tc + 1) * 512],
                             start=(k == 0), stop=(k == KT - 1)))())
                ssl = "siluT0" if it % 2 == 0 else "siluT1"
                P.op("ACT", [pa_t], [ssl],
                     (lambda pa=pa, poa=poa, ssl=ssl: lambda e: e.activation(
                         sb[ssl][:, 0:512], pa[:, poa:poa + 512], AF.Silu))())
                P.op("DVE", [ssl, pb_t], [f"gT{mh}_{tc}"],
                     (lambda pb=pb, pob=pob, ssl=ssl, mh=mh, tc=tc: lambda e: e.tensor_tensor(
                         sb["gT"][:, mh * 2048 + tc * 512: mh * 2048 + (tc + 1) * 512],
                         sb[ssl][:, 0:512], pb[:, pob:pob + 512], ALU.mult))())
                it += 1
        for t in range(TT):
            for dc in range(2):
                pz_t, pz, poz = pool[it % 6]
                it += 1
                gtoks = [f"gT{mh}_{t // 4}" for mh in range(4)]
                for k in range(4):
                    P.op("PE", gtoks + [f"sw2c{sl}"], [pz_t],
                         (lambda k=k, t=t, dc=dc, w2=w2, pz=pz, poz=poz: lambda e: e.matmul(
                             pz[:, poz:poz + 512],
                             sb["gT"][:, k * 2048 + t * 128: k * 2048 + (t + 1) * 128],
                             w2[:, k * 1024 + dc * 512: k * 1024 + dc * 512 + 512],
                             start=(k == 0), stop=(k == 3)))())
                P.op("DVE", [pz_t, f"x{t}"], [f"x{t}"],
                     (lambda pz=pz, poz=poz, t=t, dc=dc: lambda e: e.scalar_tensor_tensor(
                         x[:, t * 1024 + dc * 512: t * 1024 + dc * 512 + 512],
                         pz[:, poz:poz + 512], 0.5,
                         x[:, t * 1024 + dc * 512: t * 1024 + dc * 512 + 512],
                         ALU.mult, ALU.add))())

    P.alias(SHARED_TOKS, ROUTED_TOKS)
    for eidx in range(NEXP):
        esl = eidx % 2
        ew2 = sb[f"ew2_{esl}"]
        P.dma("POOL", [], ["ew1"],
              (lambda eidx=eidx: lambda e: e.dma_start(
                  sb["ew1"][:, :], dr["ew1"][:, eidx * 8192:(eidx + 1) * 8192]))())
        P.dma("POOL", [], [f"ew2_{esl}"],
              (lambda eidx=eidx, ew2=ew2: lambda e: e.dma_start(
                  ew2[:, :], dr["ew2"][:, eidx * 8192:(eidx + 1) * 8192]))())
        for half in range(2):
            hsl = f"hT{(eidx * 2 + half) % 2}"
            hT = sb[hsl]
            xtoks = [f"xnT{half * 8 + j}" for j in range(8)]
            for mh in range(8):
                pp = P0 if mh % 2 == 0 else P1
                ptoks = ["P0a", "P0b"] if mh % 2 == 0 else ["P1a", "P1b"]
                for nc2 in range(2):
                    for k in range(KT):
                        P.op("PE", xtoks + ["ew1"], [ptoks[nc2]],
                             (lambda k=k, mh=mh, half=half, nc2=nc2, pp=pp: lambda e: e.matmul(
                                 pp[:, nc2 * 512:(nc2 + 1) * 512],
                                 sb["ew1"][:, k * 1024 + mh * 128: k * 1024 + (mh + 1) * 128],
                                 sb["xnT"][:, k * 2048 + half * 1024 + nc2 * 512:
                                           k * 2048 + half * 1024 + (nc2 + 1) * 512],
                                 start=(k == 0), stop=(k == KT - 1)))())
                P.op("ACT", ptoks, [f"{hsl}_{mh}"],
                     (lambda mh=mh, pp=pp, hT=hT: lambda e: e.activation(
                         hT[:, mh * 1024:(mh + 1) * 1024], pp[:, :], AF.Gelu))())
            ypool = [("P2a", P2, 0), ("P2b", P2, 512), ("P3", P3, 0)]
            it = 0
            for tl in range(8):
                t = half * 8 + tl
                for dc in range(2):
                    pz_t, pz, poz = ypool[it % 3]
                    it += 1
                    htoks = [f"{hsl}_{mh}" for mh in range(8)]
                    for k in range(KT):
                        P.op("PE", htoks + [f"ew2_{esl}"], [pz_t],
                             (lambda k=k, tl=tl, dc=dc, pz=pz, poz=poz, hT=hT, ew2=ew2: lambda e: e.matmul(
                                 pz[:, poz:poz + 512],
                                 hT[:, k * 1024 + tl * 128: k * 1024 + (tl + 1) * 128],
                                 ew2[:, k * 1024 + dc * 512: k * 1024 + dc * 512 + 512],
                                 start=(k == 0), stop=(k == KT - 1)))())
                    P.op("DVE", [pz_t, f"x{t}", "comb05"], [f"x{t}"],
                         (lambda pz=pz, poz=poz, t=t, dc=dc, eidx=eidx: lambda e: e.scalar_tensor_tensor(
                             x[:, t * 1024 + dc * 512: t * 1024 + dc * 512 + 512],
                             pz[:, poz:poz + 512],
                             sb["comb05"][:, t * 8 + eidx: t * 8 + eidx + 1],
                             x[:, t * 1024 + dc * 512: t * 1024 + dc * 512 + 512],
                             ALU.mult, ALU.add))())



def attn_sublayer(P, dr, sb, psums):
    P0, P1, P2, P3, PTp = psums
    x, st = sb["x"], sb["stats"]
    KTb, V, wkv, wpj = sb["KTb"], sb["V"], sb["wkv"], sb["wpj"]
    xnTq, QTq, OTq, Ebuf, Bsb = sb["xnTq"], sb["QTq"], sb["OTq"], sb["Ebuf"], sb["Bsb"]

    ln_stats(P, sb, "ceps1", ("xnma",))

    def xn_strips(tc):
        for j in range(4):
            t = tc * 4 + j
            xn_tile(P, sb, t, "xnma", sb["xnma"][:, :])
            dst = (lambda j=j: xnTq[:, :].rearrange("p (k s) -> p k s", s=512)[:, :, j * 128:(j + 1) * 128])
            transpose_tile(P, sb, psums, "xnma", sb["xnma"], "xnTq", dst)

    qpool = [("P2a", P2, 0), ("P2b", P2, 512), ("P3", P3, 0)]

    if "kv" not in ATTN_STAGES:
        return
    P.dma("POOL", ["wkv"], ["wkv"], lambda e: e.dma_start(wkv[:, :], dr["wk"][:, :]))
    it = 0
    for tc in range(4):
        xn_strips(tc)
        for mh in range(8):
            pz_t, pz, poz = qpool[it % 3]
            it += 1
            for k in range(KT):
                P.op("PE", ["xnTq", "wkv"], [pz_t],
                     (lambda k=k, mh=mh, pz=pz, poz=poz: lambda e: e.matmul(
                         pz[:, poz:poz + 512],
                         wkv[:, k * 1024 + mh * 128: k * 1024 + (mh + 1) * 128],
                         xnTq[:, k * 512:(k + 1) * 512],
                         start=(k == 0), stop=(k == KT - 1)))())
            P.op("DVE", [pz_t], [f"KTc{tc}"],
                 (lambda pz=pz, poz=poz, mh=mh, tc=tc: lambda e: e.tensor_copy(
                     KTb[:, mh * 2048 + tc * 512: mh * 2048 + (tc + 1) * 512],
                     pz[:, poz:poz + 512]))())

    P.dma("POOL", ["wkv"], ["wkv"], lambda e: e.dma_start(wkv[:, :], dr["wv"][:, :]))
    for t in range(TT):
        P.op("DVE", [], ["Vones"],
             (lambda t=t: lambda e: e.memset(
                 V[:, t * 1040: (t + 1) * 1040].rearrange("p (h c) -> p h c", c=65)[:, :, 64:65],
                 1.0))())
    it = 0
    for tc in range(4):
        xn_strips(tc)
        for j in range(4):
            t = tc * 4 + j
            for nc2 in range(2):
                pz_t, pz, poz = qpool[it % 3]
                it += 1
                for k in range(KT):
                    P.op("PE", ["xnTq", "wkv"], [pz_t],
                         (lambda k=k, j=j, nc2=nc2, pz=pz, poz=poz: lambda e: e.matmul(
                             pz[:, poz:poz + 512],
                             xnTq[:, k * 512 + j * 128: k * 512 + (j + 1) * 128],
                             wkv[:, k * 1024 + nc2 * 512: k * 1024 + nc2 * 512 + 512],
                             start=(k == 0), stop=(k == KT - 1)))())
                P.op("DVE", [pz_t, "Vones"], [f"Vt{t}"],
                     (lambda pz=pz, poz=poz, t=t, nc2=nc2: lambda e: e.tensor_copy(
                         V[:, t * 1040 + nc2 * 520: t * 1040 + nc2 * 520 + 520]
                         .rearrange("p (h c) -> p h c", c=65)[:, :, 0:64],
                         pz[:, poz:poz + 512].rearrange("p (h c) -> p h c", c=64)))())

    if "qt" not in ATTN_STAGES:
        return
    P.dma("POOL", ["wkv"], ["wkv"], lambda e: e.dma_start(wkv[:, :], dr["wq"][:, :]))
    P.dma("POOL", [], ["wpj"], lambda e: e.dma_start(wpj[:, :], dr["wproj"][:, :]))
    for qc in range(4):
        xn_strips(qc)
        for mh in range(8):
            for k in range(KT):
                P.op("PE", ["xnTq", "wkv"], ["P3"],
                     (lambda k=k, mh=mh: lambda e: e.matmul(
                         P3[:, :], wkv[:, k * 1024 + mh * 128: k * 1024 + (mh + 1) * 128],
                         xnTq[:, k * 512:(k + 1) * 512],
                         start=(k == 0), stop=(k == KT - 1)))())
            P.op("DVE", ["P3"], [f"QT{mh}"],
                 (lambda mh=mh: lambda e: e.tensor_copy(QTq[:, mh * 512:(mh + 1) * 512], P3[:, :]))())

        if "scores" not in ATTN_STAGES:
            continue
        ktoks = [f"KTc{c}" for c in range(4)]
        for pair in range(8):
            for k2 in range(8):
                for hh in range(2):
                    pp = P0 if hh == 0 else P1
                    pt = ["P0a", "P0b"] if hh == 0 else ["P1a", "P1b"]
                    lo = hh * 64
                    for dk in range(2):
                        k = k2 * 2 + dk
                        P.op("PE", ktoks + [f"QT{pair}"], [pt[dk]],
                             (lambda pair=pair, k=k, lo=lo, dk=dk, pp=pp: lambda e: e.matmul(
                                 pp[:, dk * 512:(dk + 1) * 512],
                                 KTb[lo:lo + 64, pair * 2048 + k * 128: pair * 2048 + (k + 1) * 128],
                                 QTq[lo:lo + 64, pair * 512:(pair + 1) * 512],
                                 start=True, stop=True))())
                    P.op("ACT", pt, [f"E{hh}"],
                         (lambda pp=pp, hh=hh: lambda e: e.activation(
                             Ebuf[:, hh * 1024:(hh + 1) * 1024], pp[:, :], AF.Exp, scale=0.125))())
                for hh in range(2):
                    h = pair * 2 + hh
                    po = P2
                    po_t = "P2a" if hh == 0 else "P2b"
                    poo = hh * 512
                    for dk in range(2):
                        k = k2 * 2 + dk
                        P.op("PE", [f"E{hh}", f"Vt{k}"], [po_t],
                             (lambda k=k, h=h, dk=dk, hh=hh, poo=poo: lambda e: e.matmul(
                                 P2[0:65, poo:poo + 512],
                                 V[:, k * 1040 + h * 65: k * 1040 + (h + 1) * 65],
                                 Ebuf[:, hh * 1024 + dk * 512: hh * 1024 + (dk + 1) * 512],
                                 start=(k == 0), stop=(k == 15)))())
            for hh in range(2):
                h = pair * 2 + hh
                poo = hh * 512
                po_t = "P2a" if hh == 0 else "P2b"
                P.op("DVE", [po_t], ["rfull"],
                     (lambda poo=poo: lambda e: e.reciprocal(
                         sb["rfull"][64:65, :], P2[64:65, poo:poo + 512]))())
                P.op("PE", ["rfull", "onesb"], ["P3"],
                     lambda e: e.matmul(P3[0:64, :], sb["onesb"][64:65, 0:64],
                                        sb["rfull"][64:65, :], start=True, stop=True,
                                        tile_position=(64, 0)))
                P.op("DVE", ["P3"], ["Bsb"],
                     lambda e: e.tensor_copy(Bsb[0:64, :], P3[0:64, :]))
                P.op("DVE", [po_t, "Bsb"], [f"OT{h}"],
                     (lambda poo=poo, h=h: lambda e: e.tensor_tensor(
                         OTq[(h % 2) * 64:(h % 2) * 64 + 64, (h // 2) * 512:(h // 2 + 1) * 512],
                         P2[0:64, poo:poo + 512], Bsb[0:64, :], ALU.mult))())

        plevel = 4
        for sflag in ATTN_STAGES:
            if sflag.startswith("proj") and len(sflag) > 4:
                plevel = int(sflag[4:])
        if not any(s.startswith("proj") for s in ATTN_STAGES):
            continue
        otoks = [f"OT{h}" for h in range(16)]
        gsc = sb["gsc"]
        pA, tA = P0, ["P0a", "P0b"]
        pB, tB = P1, ["P1a", "P1b"]
        for j in range(4):
            t = qc * 4 + j
            for dc in range(2):
                for k in range(KT):
                    P.op("PE", otoks + ["wpj"], [tA[dc]],
                         (lambda k=k, j=j, dc=dc: lambda e: e.matmul(
                             pA[:, dc * 512:(dc + 1) * 512],
                             OTq[:, k * 512 + j * 128: k * 512 + (j + 1) * 128],
                             wpj[:, k * 1024 + dc * 512: k * 1024 + dc * 512 + 512],
                             start=(k == 0), stop=(k == KT - 1)))())
            if plevel == 1:
                P.op("DVE", tA + [f"x{t}"], [f"x{t}"],
                     (lambda t=t: lambda e: e.tensor_tensor(
                         x[:, t * 1024:(t + 1) * 1024], x[:, t * 1024:(t + 1) * 1024],
                         pA[:, :], ALU.add))())
                continue
            P.op("DVE", tA + [f"x{t}", "st_rstd"], tB,
                 (lambda t=t: lambda e: e.scalar_tensor_tensor(
                     pB[:, :], x[:, t * 1024:(t + 1) * 1024], st[:, 32 + t:33 + t],
                     pA[:, :], ALU.mult, ALU.add))())
            P.op("DVE", tB + ["st_mrs"], tA,
                 (lambda t=t: lambda e: e.tensor_scalar_sub(
                     pA[:, :], pB[:, :], st[:, 48 + t:49 + t]))())
            P.op("DVE", tA + ["bpj"], tB,
                 lambda e: e.tensor_tensor(pB[:, :], pA[:, :], sb["bpj"][:, :], ALU.add))
            if plevel == 2:
                P.op("DVE", tB + [f"x{t}"], [f"x{t}"],
                     (lambda t=t: lambda e: e.tensor_tensor(
                         x[:, t * 1024:(t + 1) * 1024], x[:, t * 1024:(t + 1) * 1024],
                         pB[:, :], ALU.add))())
                continue
            P.op("DVE", tB, ["l2s"],
                 lambda e: e.reduce_sum(gsc[:, 112:113], pB[:, :], axis=AX.X))
            if plevel == 35:
                P.op("DVE", [], ["l2q"],
                     lambda e: e.memset(gsc[:, 113:114], 1.0))
            elif plevel == 37:
                P.op("ACT", tB + ["PT"], ["l2q", "PT"],
                     lambda e: e.activation(
                         PTp[:, :], pB[:, :], AF.Square, accum_out=gsc[:, 113:114]))
            elif plevel == 41:
                P.op("ACT", tB, ["l2q", "xnma"],
                     lambda e: e.activation(
                         sb["xnma"][:, :], pB[:, :], AF.Exp, accum_out=gsc[:, 113:114]))
            elif plevel == 42:
                P.op("ACT", tB, ["xnma"],
                     lambda e: e.activation(sb["xnma"][:, :], pB[:, :], AF.Copy))
                P.op("ACT", ["xnma", "PT"], ["l2q", "PT"],
                     lambda e: e.activation(
                         PTp[:, :], sb["xnma"][:, :], AF.Square, accum_out=gsc[:, 113:114]))
            elif plevel == 39:
                P.op("ACT", tB, ["xnma"],
                     lambda e: e.activation(sb["xnma"][:, :], pB[:, :], AF.Square))
                P.op("DVE", ["xnma"], ["l2q"],
                     lambda e: e.reduce_sum(gsc[:, 113:114], sb["xnma"][:, :], axis=AX.X))
            else:
                P.op("ACT", tB, ["l2q", "xnma"],
                     lambda e: e.activation(
                         sb["xnma"][:, :], pB[:, :], AF.Square, accum_out=gsc[:, 113:114]))
            P.op("DVE", ["l2s"], ["l2m"],
                 lambda e: e.tensor_scalar_mul(gsc[:, 112:113], gsc[:, 112:113], 1.0 / 1024.0))
            P.op("DVE", ["l2q"], ["l2e"],
                 lambda e: e.tensor_scalar_mul(gsc[:, 113:114], gsc[:, 113:114], 1.0 / 1024.0))
            P.op("DVE", ["l2m"], ["l2ms"],
                 lambda e: e.tensor_mul(gsc[:, 114:115], gsc[:, 112:113], gsc[:, 112:113]))
            P.op("DVE", ["l2e", "l2ms"], ["l2v"],
                 lambda e: e.tensor_sub(gsc[:, 113:114], gsc[:, 113:114], gsc[:, 114:115]))
            if plevel == 36:
                P.op("DVE", ["l2v"], ["l2sd"],
                     lambda e: e.tensor_scalar_add(gsc[:, 114:115], gsc[:, 113:114], EPS2))
            else:
                P.op("ACT", ["l2v", "ceps2"], ["l2sd"],
                     lambda e: e.activation(gsc[:, 114:115], gsc[:, 113:114], AF.Sqrt,
                                            bias=sb["ceps2"][:, :]))
            P.op("DVE", ["l2sd"], ["l2r"],
                 lambda e: e.reciprocal(gsc[:, 114:115], gsc[:, 114:115]))
            P.op("DVE", ["l2r", "l2m"], ["l2mr"],
                 lambda e: e.tensor_mul(gsc[:, 115:116], gsc[:, 112:113], gsc[:, 114:115]))
            if plevel in (3, 35, 36, 37, 39, 41, 42):
                P.op("DVE", tB + [f"x{t}"], [f"x{t}"],
                     (lambda t=t: lambda e: e.tensor_tensor(
                         x[:, t * 1024:(t + 1) * 1024], x[:, t * 1024:(t + 1) * 1024],
                         pB[:, :], ALU.add))())
                continue
            P.op("DVE", tB + ["l2r", "l2mr"], tA,
                 lambda e: e.tensor_scalar(
                     pA[:, :], pB[:, :], gsc[:, 114:115], gsc[:, 115:116],
                     ALU.mult, ALU.subtract))
            P.op("DVE", tA + ["gam"], tB,
                 lambda e: e.tensor_tensor(pB[:, :], pA[:, :], sb["gam"][:, :], ALU.mult))
            P.op("DVE", tB + ["bet"], tA,
                 lambda e: e.tensor_tensor(pA[:, :], pB[:, :], sb["bet"][:, :], ALU.add))
            P.op("DVE", tA + [f"x{t}"], [f"x{t}"],
                 (lambda t=t: lambda e: e.tensor_tensor(
                     x[:, t * 1024:(t + 1) * 1024], x[:, t * 1024:(t + 1) * 1024],
                     pA[:, :], ALU.add))())



def _tile_rows(a, r=128):
    R, C = a.shape
    return np.ascontiguousarray(
        a.reshape(R // r, r, C).transpose(1, 0, 2).reshape(r, (R // r) * C))


def prep_inputs(L0_attn_x, L0_attn_w_q, L0_attn_w_k, L0_attn_w_v, L0_attn_w_proj,
                L0_attn_b_proj, L0_attn_ln_gamma, L0_attn_ln_beta,
                L0_ffn_router_w, L0_ffn_router_bias, L0_ffn_sw1, L0_ffn_sw3,
                L0_ffn_sw2, L0_ffn_ew1, L0_ffn_ew2, nb=NB, ncores=NCORES):
    xt = np.asarray(L0_attn_x, dtype=np.float32)

    def rep(v):
        v = np.asarray(v, np.float32)
        return np.ascontiguousarray(np.broadcast_to(v[None, :], (128, v.shape[0])))

    common = dict(
        wq=_tile_rows(np.asarray(L0_attn_w_q, np.float32)),
        wk=_tile_rows(np.asarray(L0_attn_w_k, np.float32)),
        wv=_tile_rows(np.asarray(L0_attn_w_v, np.float32)),
        wproj=_tile_rows(np.asarray(L0_attn_w_proj, np.float32)),
        sw1=np.ascontiguousarray(
            np.asarray(L0_ffn_sw1, np.float32).reshape(8, 128, 4, 512)
            .transpose(1, 2, 0, 3).reshape(128, 16384)),
        sw3=np.ascontiguousarray(
            np.asarray(L0_ffn_sw3, np.float32).reshape(8, 128, 4, 512)
            .transpose(1, 2, 0, 3).reshape(128, 16384)),
        sw2=np.ascontiguousarray(
            np.asarray(L0_ffn_sw2, np.float32).reshape(4, 4, 128, 1024)
            .transpose(2, 0, 1, 3).reshape(128, 16384)),
        ew1=np.ascontiguousarray(
            np.asarray(L0_ffn_ew1, np.float32).reshape(8, 8, 128, 1024)
            .transpose(2, 0, 1, 3).reshape(128, 8 * 8192)),
        ew2=np.ascontiguousarray(
            np.asarray(L0_ffn_ew2, np.float32).reshape(8, 8, 128, 1024)
            .transpose(2, 0, 1, 3).reshape(128, 8 * 8192)),
        rw=np.ascontiguousarray(
            np.asarray(L0_ffn_router_w, np.float32).T.reshape(8, 128, 8)
            .transpose(1, 0, 2).reshape(128, 64)),
        bpj=rep(L0_attn_b_proj), gam=rep(L0_attn_ln_gamma),
        bet=rep(L0_attn_ln_beta), rb=rep(L0_ffn_router_bias),
    )
    in_maps = []
    for c in range(ncores):
        xs = xt[c * nb:(c + 1) * nb]
        xs = np.ascontiguousarray(
            xs.reshape(nb, TT, 128, 1024).transpose(2, 0, 1, 3).reshape(128, nb * TT * 1024))
        m = dict(common)
        m["xin"] = xs
        in_maps.append(m)
    return in_maps


def kernel(**inputs):
    in_maps = prep_inputs(**inputs)
    nc = build_nc()
    res = run_bass_kernel_spmd(nc, in_maps, core_ids=list(range(NCORES)))
    outs = []
    for c in range(NCORES):
        yc = np.asarray(res.results[c]["y"])
        yc = yc.reshape(128, NB, TT, 1024).transpose(1, 2, 0, 3).reshape(NB, S, D)
        outs.append(yc)
    return np.concatenate(outs, axis=0).astype(np.float32)


# version 06
# speedup vs baseline: 1.1886x; 1.1886x over previous
import numpy as np
import ml_dtypes
import concourse.bass as bass
import concourse.mybir as mybir
from concourse.bass_utils import run_bass_kernel_spmd

F32 = mybir.dt.float32
BF16 = mybir.dt.bfloat16
AF = mybir.ActivationFunctionType
ALU = mybir.AluOpType
AX = mybir.AxisListType

NCORES = 8
NB = 4
S = 2048
D = 1024
TT = 16
KT = 8
NEXP = 8
ATTN_STAGES = {"kv", "qt", "scores", "proj"}
EPS1 = 1e-5
EPS2 = 1e-6


ENGS = ("PE", "ACT", "DVE", "POOL", "SP")
_BLOCK_ATTR = {"PE": "tensor", "ACT": "scalar", "DVE": "vector", "POOL": "gpsimd", "SP": "sync"}
N_LANES = 4
ELIDE_SAME_ENGINE = ("PE",)


class Sched:
    def __init__(self):
        self.ops = []
        self.tok = {}
        self.lane_rr = {}
        self.lane_cnt = {}
        self.lane_last = {}

    def _t(self, t):
        d = self.tok.get(t)
        if d is None:
            d = {"w": None, "r": {}, "pend": None}
            self.tok[t] = d
        return d

    EXCLUSIVE = frozenset(["P0a", "P0b", "P1a", "P1b", "P2a", "P2b", "P3", "PT"])

    def _record(self, eng, kind, reads, writes, emit, queue=None):
        i = len(self.ops)
        excl = [t for t in reads if t in self.EXCLUSIVE]
        if excl:
            reads = [t for t in reads if t not in self.EXCLUSIVE]
            writes = list(writes) + excl
        deps = set()
        for t in reads:
            d = self._t(t)
            if d["pend"]:
                deps |= d["pend"]
            if d["w"] is not None:
                deps.add(d["w"])
        for t in writes:
            d = self._t(t)
            if d["pend"]:
                deps |= d["pend"]
            if d["w"] is not None:
                deps.add(d["w"])
            deps |= set(d["r"].values())
        op = {
            "i": i, "eng": eng, "kind": kind, "deps": deps, "emit": emit,
            "queue": queue, "lane": None, "lcnt": None, "inc": False, "inc_id": None,
        }
        self.ops.append(op)
        if kind == "d":
            q = queue
            lane = self.lane_rr.get(q, 0)
            self.lane_rr[q] = (lane + 1) % N_LANES
            cnt = self.lane_cnt.get((q, lane), 0) + 1
            self.lane_cnt[(q, lane)] = cnt
            op["lane"] = lane
            op["lcnt"] = cnt
            prev = self.lane_last.get((q, lane))
            if prev is not None:
                deps.add(prev)
            self.lane_last[(q, lane)] = i
        rk = ("c", eng) if kind == "c" else ("d", queue, op["lane"])
        for t in reads:
            self._t(t)["r"][rk] = i
        for t in writes:
            self.tok[t] = {"w": i, "r": {}, "pend": None}
        deps.discard(i)
        return i

    def op(self, eng, reads, writes, emit):
        return self._record(eng, "c", reads, writes, emit)

    def dma(self, queue, reads, writes, emit):
        return self._record(queue, "d", reads, writes, emit, queue=queue)

    def alias(self, dying, new):
        deps = set()
        for t in dying:
            d = self.tok.get(t)
            if d is None:
                continue
            if d["w"] is not None:
                deps.add(d["w"])
            deps |= set(d["r"].values())
            if d["pend"]:
                deps |= d["pend"]
            del self.tok[t]
        for t in new:
            d = self.tok.get(t)
            old = set(deps)
            if d is not None:
                if d["w"] is not None:
                    old.add(d["w"])
                old |= set(d["r"].values())
                if d["pend"]:
                    old |= d["pend"]
            self.tok[t] = {"w": None, "r": {}, "pend": old}

    def finalize(self, nc):
        ops = self.ops
        for op in ops:
            pruned = set()
            for d in op["deps"]:
                p = ops[d]
                if (p["kind"] == "c" and op["kind"] == "c" and p["eng"] == op["eng"]
                        and p["eng"] in ELIDE_SAME_ENGINE):
                    continue
                pruned.add(d)
            op["deps"] = pruned
            for d in pruned:
                p = ops[d]
                if p["kind"] == "c":
                    p["inc"] = True
        last_per = {}
        for op in ops:
            if op["kind"] == "c":
                last_per[("c", op["eng"])] = op["i"]
            else:
                last_per[("d", op["queue"], op["lane"])] = op["i"]
        drain_deps = set(last_per.values())
        drain = {"i": len(ops), "eng": "SP", "kind": "c", "deps": drain_deps,
                 "emit": None, "queue": None, "lane": None, "lcnt": None,
                 "inc": False, "inc_id": None}
        ops.append(drain)
        for d in drain_deps:
            if ops[d]["kind"] == "c":
                ops[d]["inc"] = True
        cnt = {e: 0 for e in ENGS}
        for op in ops:
            if op["kind"] == "c" and op["inc"]:
                cnt[op["eng"]] += 1
                op["inc_id"] = cnt[op["eng"]]
        sem_eng = {e: nc.alloc_semaphore(name=f"se_{e}") for e in ENGS}
        sem_lane = {}
        for (q, lane) in self.lane_cnt:
            sem_lane[(q, lane)] = nc.alloc_semaphore(name=f"sd_{q}_{lane}")
        clock = {e: {} for e in ENGS}
        snap = {}
        for op in ops:
            ek = op["eng"] if op["kind"] == "c" else op["queue"]
            c = clock[ek]
            need = {}
            prod = {}
            for d in op["deps"]:
                p = ops[d]
                if p["kind"] == "c":
                    if (op["kind"] == "c" and p["eng"] == op["eng"]
                            and p["eng"] in ELIDE_SAME_ENGINE):
                        continue
                    sk = ("e", p["eng"])
                    v = p["inc_id"]
                else:
                    sk = ("l", p["queue"], p["lane"])
                    v = 16 * p["lcnt"]
                if v is None:
                    continue
                if need.get(sk, -1) < v:
                    need[sk] = v
                    prod[sk] = d
            waits = []
            for sk, v in sorted(need.items(), key=lambda kv: str(kv[0])):
                if c.get(sk, 0) >= v:
                    continue
                waits.append((sk, v))
                ps = snap.get(prod[sk])
                if ps:
                    for k2, v2 in ps.items():
                        if c.get(k2, 0) < v2:
                            c[k2] = v2
                if c.get(sk, 0) < v:
                    c[sk] = v
            op["waits"] = waits
            s = dict(c)
            if op["kind"] == "c":
                if op["inc"]:
                    s[("e", op["eng"])] = op["inc_id"]
            else:
                s[("l", op["queue"], op["lane"])] = 16 * op["lcnt"]
            snap[op["i"]] = s
        self.sem_eng = sem_eng
        self.sem_lane = sem_lane

    def emit(self, nc):
        per_eng = {e: [] for e in ENGS}
        for op in self.ops:
            ek = op["eng"] if op["kind"] == "c" else op["queue"]
            per_eng[ek].append(op)
        sem_eng, sem_lane = self.sem_eng, self.sem_lane

        def run(eng_obj, ops_list):
            for op in ops_list:
                for sk, v in op["waits"]:
                    sem = sem_eng[sk[1]] if sk[0] == "e" else sem_lane[(sk[1], sk[2])]
                    eng_obj.wait_ge(sem, v)
                if op["emit"] is None:
                    continue
                inst = op["emit"](eng_obj)
                if op["kind"] == "d":
                    inst.then_inc(sem_lane[(op["queue"], op["lane"])], 16)
                elif op["inc"]:
                    inst.then_inc(sem_eng[op["eng"]], 1)

        with nc.Block() as block:
            for e in ENGS:
                if not per_eng[e]:
                    continue
                getattr(block, _BLOCK_ATTR[e])(
                    (lambda lst: (lambda eng_obj: run(eng_obj, lst)))(per_eng[e])
                )



SB = {}


def _place(name, off, shape, dt):
    off = (off + 31) & ~31
    SB[name] = (off, shape, dt)
    return off + shape[1] * (4 if dt == F32 else 2)


def build_mem_map():
    SB.clear()
    o = 0
    o = _place("x", o, [128, TT * 1024], F32)
    o = _place("gam", o, [128, 1024], F32)
    o = _place("bet", o, [128, 1024], F32)
    o = _place("bpj", o, [128, 1024], F32)
    o = _place("rbias", o, [128, 8], F32)
    o = _place("rwT", o, [128, 64], BF16)
    o = _place("ident", o, [128, 128], BF16)
    o = _place("onesb", o, [128, 64], BF16)
    o = _place("chalf", o, [128, 1], F32)
    o = _place("ceps1", o, [128, 1], F32)
    o = _place("ceps2", o, [128, 1], F32)
    o = _place("stats", o, [128, 64], F32)
    o = _place("gsc", o, [128, 128], F32)
    o = _place("s_sb", o, [128, 128], F32)
    o = _place("comb05", o, [128, 128], F32)
    o = _place("rfull", o, [128, 512], BF16)
    o = (o + 31) & ~31
    base = o
    _place("xnT", base, [128, 16384], BF16)
    for sl in (0, 1):
        w0 = base + 32768 + sl * 24576
        w0 = _place(f"sw1c{sl}", w0, [128, 4096], BF16)
        w0 = _place(f"sw3c{sl}", w0, [128, 4096], BF16)
        w0 = _place(f"sw2c{sl}", w0, [128, 4096], BF16)
    _place("gT", base + 81920, [128, 8192], BF16)
    _place("sel", base + 81920, [128, 128], F32)
    _place("m1", base + 82432, [128, 128], F32)
    _place("m2", base + 82944, [128, 128], F32)
    _place("siluT0", base + 98304, [128, 1024], BF16)
    _place("siluT1", base + 100352, [128, 1024], BF16)
    _place("xnm0", base + 102400, [128, 1024], BF16)
    _place("xnm1", base + 104448, [128, 1024], BF16)
    _place("ew1", base + 32768, [128, 8192], BF16)
    _place("ew2_0", base + 49152, [128, 8192], BF16)
    _place("ew2_1", base + 65536, [128, 8192], BF16)
    _place("hT0", base + 81920, [128, 8192], BF16)
    _place("hT1", base + 98304, [128, 8192], BF16)
    _place("KTb", base + 0, [128, 16384], BF16)
    _place("V", base + 32768, [128, 16640], BF16)
    _place("wkv", base + 66048, [128, 8192], BF16)
    _place("wpj", base + 82432, [128, 8192], BF16)
    _place("xnTq", base + 98816, [128, 4096], BF16)
    _place("QTq", base + 107008, [128, 4096], BF16)
    _place("OTq", base + 115200, [128, 4096], BF16)
    _place("Ebuf", base + 123392, [128, 2048], BF16)
    _place("Bsb", base + 127488, [128, 512], BF16)
    _place("xnma", base + 128512, [128, 1024], BF16)
    return base + 130560


SHARED_TOKS = (["sel_t", "m1_t", "m2_t"]
               + [f"sw{w}c{s}" for w in (1, 3, 2) for s in (0, 1)]
               + [f"gT{m}_{c}" for m in range(4) for c in range(4)]
               + ["siluT0", "siluT1", "xnm0", "xnm1"])
ROUTED_TOKS = (["ew1", "ew2_0", "ew2_1"]
               + [f"hT0_{m}" for m in range(8)] + [f"hT1_{m}" for m in range(8)])
FFN_TOKS = [f"xnT{t}" for t in range(TT)] + SHARED_TOKS + ROUTED_TOKS
ATT_TOKS = ([f"KTc{c}" for c in range(4)] + [f"Vt{t}" for t in range(TT)]
            + ["Vones", "wkv", "wpj", "xnTq"] + [f"QT{m}" for m in range(8)]
            + [f"OT{h}" for h in range(16)] + ["E0", "E1", "Bsb", "xnma"])



def build_nc(nb=NB, sublayers=("f", "a", "f")):
    nc = bass.Bass("TRN2", target_bir_lowering=False)
    build_mem_map()

    dr = {}
    dr["xin"] = nc.dram_tensor("xin", [128, nb * TT * 1024], F32, kind="ExternalInput")
    dr["y"] = nc.dram_tensor("y", [128, nb * TT * 1024], F32, kind="ExternalOutput")
    for n, f in [("wq", 8192), ("wk", 8192), ("wv", 8192), ("wproj", 8192),
                 ("sw1", 16384), ("sw3", 16384), ("sw2", 16384),
                 ("ew1", 8 * 8192), ("ew2", 8 * 8192)]:
        dr[n] = nc.dram_tensor(n, [128, f], F32, kind="ExternalInput")
    dr["bpj"] = nc.dram_tensor("bpj", [128, 1024], F32, kind="ExternalInput")
    dr["gam"] = nc.dram_tensor("gam", [128, 1024], F32, kind="ExternalInput")
    dr["bet"] = nc.dram_tensor("bet", [128, 1024], F32, kind="ExternalInput")
    dr["rw"] = nc.dram_tensor("rw", [128, 64], F32, kind="ExternalInput")
    dr["rb"] = nc.dram_tensor("rb", [128, 8], F32, kind="ExternalInput")
    dr["identc"] = nc.inline_tensor(np.eye(128, dtype=ml_dtypes.bfloat16), name="identc")

    base0 = (229376 - nc.sbuf_bytes_remaining + 31) & ~31
    sb = {}
    for name, (off, shape, dt) in SB.items():
        sb[name] = nc.alloc_sbuf_tensor_at(name, shape, dt, offset=base0 + off)

    psums = (
        nc.alloc_psum_tensor("P0", [128, 1024], F32),
        nc.alloc_psum_tensor("P1", [128, 1024], F32),
        nc.alloc_psum_tensor("P2", [128, 1024], F32),
        nc.alloc_psum_tensor("P3", [128, 512], F32),
        nc.alloc_psum_tensor("PTp", [128, 1024], BF16),
    )

    P = Sched()
    build_program(P, dr, sb, psums, nb=nb, sublayers=sublayers)
    P.finalize(nc)
    with nc.allow_low_precision(reason="bf16 matmul operand stores"):
        P.emit(nc)
    return nc


def warm(P, sb, psums):
    P3 = psums[3]
    P.op("PE", ["onesb"], ["P3"],
         lambda e: e.matmul(P3[0:64, 0:64], sb["onesb"][0:1, 0:64],
                            sb["onesb"][0:1, 0:64], start=True, stop=True))


def build_program(P, dr, sb, psums, nb=NB, sublayers=("f", "a", "f")):
    x = sb["x"]
    P.dma("SP", [], ["gam"], lambda e: e.dma_start(sb["gam"][:, :], dr["gam"][:, :]))
    P.dma("SP", [], ["bet"], lambda e: e.dma_start(sb["bet"][:, :], dr["bet"][:, :]))
    P.dma("SP", [], ["bpj"], lambda e: e.dma_start(sb["bpj"][:, :], dr["bpj"][:, :]))
    P.dma("SP", [], ["rbias"], lambda e: e.dma_start(sb["rbias"][:, :], dr["rb"][:, :]))
    P.dma("SP", [], ["ident"], lambda e: e.dma_start(sb["ident"][:, :], dr["identc"].ap()))
    P.dma("POOL", [], ["rwT"], lambda e: e.dma_start(sb["rwT"][:, :], dr["rw"][:, :]))
    P.op("DVE", [], ["onesb"], lambda e: e.memset(sb["onesb"][:, :], 1.0))
    P.op("DVE", [], ["chalf"], lambda e: e.memset(sb["chalf"][:, :], 0.5))
    P.op("DVE", [], ["ceps1"], lambda e: e.memset(sb["ceps1"][:, :], EPS1))
    P.op("DVE", [], ["ceps2"], lambda e: e.memset(sb["ceps2"][:, :], EPS2))

    prev = None
    for b in range(nb):
        xoff = b * TT * 1024
        for c4 in range(4):
            P.dma("SP", [f"x{4 * c4 + j}" for j in range(4)],
                  [f"x{4 * c4 + j}" for j in range(4)],
                  (lambda xo=xoff, c4=c4: lambda e: e.dma_start(
                      x[:, c4 * 4096:(c4 + 1) * 4096],
                      dr["xin"][:, xo + c4 * 4096: xo + (c4 + 1) * 4096]))())
        for kind in sublayers:
            if kind == "f":
                dying = ATT_TOKS if prev == "a" else (ROUTED_TOKS if prev == "f" else [])
                P.alias(dying, FFN_TOKS)
                ffn_sublayer(P, dr, sb, psums)
            else:
                P.alias(FFN_TOKS if prev == "f" else [], ATT_TOKS)
                attn_sublayer(P, dr, sb, psums)
            prev = kind
        for c4 in range(4):
            P.dma("SP", [f"x{4 * c4 + j}" for j in range(4)], [],
                  (lambda xo=xoff, c4=c4: lambda e: e.dma_start(
                      dr["y"][:, xo + c4 * 4096: xo + (c4 + 1) * 4096],
                      x[:, c4 * 4096:(c4 + 1) * 4096]))())



def ln_stats(P, sb, eps_tok, scraps, psums=None):
    x, st = sb["x"], sb["stats"]
    for t in range(TT):
        if psums is not None and t % 2 == 0:
            warm(P, sb, psums)
        xt = x[:, t * 1024:(t + 1) * 1024]
        P.op("DVE", [f"x{t}"], [f"st_s{t}"],
             (lambda xt=xt, t=t: lambda e: e.reduce_sum(st[:, t:t + 1], xt, axis=AX.X))())
        sc = scraps[t % len(scraps)]
        P.op("ACT", [f"x{t}"], [f"st_q{t}", sc],
             (lambda xt=xt, t=t, sc=sc: lambda e: e.activation(
                 sb[sc][:, :], xt, AF.Square, accum_out=st[:, 16 + t:17 + t]))())
    rs = [f"st_s{t}" for t in range(TT)] + [f"st_q{t}" for t in range(TT)]
    P.op("DVE", rs, ["st_mean"],
         lambda e: e.tensor_scalar_mul(st[:, 0:16], st[:, 0:16], 1.0 / 1024.0))
    P.op("DVE", rs, ["st_ex2"],
         lambda e: e.tensor_scalar_mul(st[:, 16:32], st[:, 16:32], 1.0 / 1024.0))
    P.op("DVE", ["st_mean"], ["st_msq"],
         lambda e: e.tensor_mul(st[:, 48:64], st[:, 0:16], st[:, 0:16]))
    P.op("DVE", ["st_ex2", "st_msq"], ["st_var"],
         lambda e: e.tensor_sub(st[:, 16:32], st[:, 16:32], st[:, 48:64]))
    P.op("ACT", ["st_var", eps_tok], ["st_std"],
         (lambda et=eps_tok: lambda e: e.activation(
             st[:, 32:48], st[:, 16:32], AF.Sqrt, bias=sb[et][:, :]))())
    P.op("DVE", ["st_std"], ["st_rstd"],
         lambda e: e.reciprocal(st[:, 32:48], st[:, 32:48]))
    P.op("DVE", ["st_rstd", "st_mean"], ["st_mrs"],
         lambda e: e.tensor_mul(st[:, 48:64], st[:, 0:16], st[:, 32:48]))


def xn_tile(P, sb, t, out_tok, out_ap):
    x, st = sb["x"], sb["stats"]
    P.op("DVE", [f"x{t}", "st_rstd", "st_mrs"], [out_tok],
         (lambda t=t, out_ap=out_ap: lambda e: e.tensor_scalar(
             out_ap, x[:, t * 1024:(t + 1) * 1024],
             st[:, 32 + t:33 + t], st[:, 48 + t:49 + t], ALU.mult, ALU.subtract))())


def transpose_tile(P, sb, psums, src_tok, src_sb, dst_tok, dst_ap_fn):
    PTp = psums[4]
    for k in range(KT):
        P.op("PE", [src_tok, "ident"], ["PT"],
             (lambda k=k, src_sb=src_sb: lambda e: e.transpose(
                 PTp[:, k * 128:(k + 1) * 128], src_sb[:, k * 128:(k + 1) * 128],
                 sb["ident"][:, :]))())
    P.op("DVE", ["PT"], [dst_tok],
         (lambda f=dst_ap_fn: lambda e: e.tensor_copy(
             f(), PTp[:, :].rearrange("p (k c) -> p k c", c=128)))())



def ffn_sublayer(P, dr, sb, psums):
    P0, P1, P2, P3, PTp = psums
    x, st, xnT = sb["x"], sb["stats"], sb["xnT"]

    ln_stats(P, sb, "ceps1", ("xnm0", "xnm1"), psums)

    for t in range(TT):
        slot = "xnm0" if t % 2 == 0 else "xnm1"
        xn_tile(P, sb, t, slot, sb[slot][:, :])
        dst = (lambda t=t: xnT[:, :].rearrange("p (k s) -> p k s", s=2048)[:, :, t * 128:(t + 1) * 128])
        transpose_tile(P, sb, psums, slot, sb[slot], f"xnT{t}", dst)

    s_sb, sel, m1, m2, gsc, comb = sb["s_sb"], sb["sel"], sb["m1"], sb["m2"], sb["gsc"], sb["comb05"]
    for t in range(TT):
        for k in range(KT):
            P.op("PE", [f"xnT{t}", "rwT"], ["P3"],
                 (lambda t=t, k=k: lambda e: e.matmul(
                     P3[:, 0:8], xnT[:, k * 2048 + t * 128: k * 2048 + (t + 1) * 128],
                     sb["rwT"][:, k * 8:(k + 1) * 8], start=(k == 0), stop=(k == KT - 1)))())
        P.op("ACT", ["P3"], [f"s{t}"],
             (lambda t=t: lambda e: e.activation(s_sb[:, t * 8:(t + 1) * 8], P3[:, 0:8], AF.Sigmoid))())
    sread = [f"s{t}" for t in range(TT)]
    s3 = s_sb[:, :].rearrange("p (t e) -> p t e", e=8)
    sel3 = sel[:, :].rearrange("p (t e) -> p t e", e=8)
    m13 = m1[:, :].rearrange("p (t e) -> p t e", e=8)
    m23 = m2[:, :].rearrange("p (t e) -> p t e", e=8)
    c3 = comb[:, :].rearrange("p (t e) -> p t e", e=8)
    rb3 = sb["rbias"][:, :].unsqueeze(1).broadcast_to([128, TT, 8])
    P.op("DVE", sread + ["rbias"], ["sel_t"], lambda e: e.tensor_tensor(sel3, s3, rb3, ALU.add))
    P.op("DVE", ["sel_t"], ["g_m1"], lambda e: e.reduce_max(gsc[:, 0:16], sel3, axis=AX.X))
    m1b = gsc[:, 0:16].unsqueeze(2).broadcast_to([128, TT, 8])
    P.op("DVE", ["sel_t", "g_m1"], ["m1_t"], lambda e: e.tensor_tensor(m13, sel3, m1b, ALU.is_equal))
    P.op("DVE", ["sel_t", "m1_t"], ["sel_t"],
         lambda e: e.scalar_tensor_tensor(sel3, m13, -1e9, sel3, ALU.mult, ALU.add))
    P.op("DVE", ["sel_t"], ["g_m2"], lambda e: e.reduce_max(gsc[:, 16:32], sel3, axis=AX.X))
    m2b = gsc[:, 16:32].unsqueeze(2).broadcast_to([128, TT, 8])
    P.op("DVE", ["sel_t", "g_m2"], ["m2_t"], lambda e: e.tensor_tensor(m23, sel3, m2b, ALU.is_equal))
    P.op("DVE", ["m1_t"] + sread, ["sel_t"], lambda e: e.tensor_tensor(sel3, m13, s3, ALU.mult))
    P.op("DVE", ["sel_t"], ["g_s1"], lambda e: e.reduce_sum(gsc[:, 32:48], sel3, axis=AX.X))
    P.op("DVE", ["m2_t"] + sread, ["sel_t"], lambda e: e.tensor_tensor(sel3, m23, s3, ALU.mult))
    P.op("DVE", ["sel_t"], ["g_s2"], lambda e: e.reduce_sum(gsc[:, 48:64], sel3, axis=AX.X))
    P.op("DVE", ["g_s1", "g_s2"], ["g_den"],
         lambda e: e.tensor_add(gsc[:, 64:80], gsc[:, 32:48], gsc[:, 48:64]))
    P.op("DVE", ["g_den"], ["g_dn9"],
         lambda e: e.tensor_scalar_add(gsc[:, 80:96], gsc[:, 64:80], 1e-9))
    P.op("DVE", ["g_dn9"], ["g_inv"], lambda e: e.reciprocal(gsc[:, 80:96], gsc[:, 80:96]))
    P.op("DVE", ["g_s1", "g_inv"], ["g_s1"],
         lambda e: e.tensor_mul(gsc[:, 32:48], gsc[:, 32:48], gsc[:, 80:96]))
    P.op("DVE", ["g_s2", "g_inv"], ["g_s2"],
         lambda e: e.tensor_mul(gsc[:, 48:64], gsc[:, 48:64], gsc[:, 80:96]))
    P.op("DVE", ["g_den"], ["g_cond"],
         lambda e: e.tensor_single_scalar(gsc[:, 96:112], gsc[:, 64:80], 1e-9, ALU.is_gt))
    for cols, tok in (((32, 48), "g_s1"), ((48, 64), "g_s2")):
        a, b = cols
        P.op("DVE", [tok], [tok],
             (lambda a=a, b=b: lambda e: e.tensor_scalar_sub(gsc[:, a:b], gsc[:, a:b], 0.5))())
        P.op("DVE", [tok, "g_cond"], [tok],
             (lambda a=a, b=b: lambda e: e.tensor_tensor(
                 gsc[:, a:b], gsc[:, a:b], gsc[:, 96:112], ALU.mult))())
        P.op("DVE", [tok], [tok],
             (lambda a=a, b=b: lambda e: e.tensor_scalar_add(gsc[:, a:b], gsc[:, a:b], 0.5))())
    g1b = gsc[:, 32:48].unsqueeze(2).broadcast_to([128, TT, 8])
    g2b = gsc[:, 48:64].unsqueeze(2).broadcast_to([128, TT, 8])
    P.op("DVE", ["m1_t", "g_s1"], ["m1_t"], lambda e: e.tensor_tensor(m13, m13, g1b, ALU.mult))
    P.op("DVE", ["m2_t", "g_s2"], ["m2_t"], lambda e: e.tensor_tensor(m23, m23, g2b, ALU.mult))
    P.op("DVE", ["m1_t", "m2_t"], ["comb_t"], lambda e: e.tensor_tensor(c3, m13, m23, ALU.add))
    P.op("DVE", ["comb_t"], ["comb05"],
         lambda e: e.tensor_scalar_mul(comb[:, :], comb[:, :], 0.5))

    pool = [("P0a", P0, 0), ("P0b", P0, 512), ("P1a", P1, 0),
            ("P1b", P1, 512), ("P2a", P2, 0), ("P2b", P2, 512)]
    for c in range(4):
        sl = c % 2
        w1, w3, w2 = sb[f"sw1c{sl}"], sb[f"sw3c{sl}"], sb[f"sw2c{sl}"]
        P.dma("POOL", [], [f"sw1c{sl}"],
              (lambda c=c, w1=w1: lambda e: e.dma_start(w1[:, :], dr["sw1"][:, c * 4096:(c + 1) * 4096]))())
        P.dma("POOL", [], [f"sw3c{sl}"],
              (lambda c=c, w3=w3: lambda e: e.dma_start(w3[:, :], dr["sw3"][:, c * 4096:(c + 1) * 4096]))())
        P.dma("POOL", [], [f"sw2c{sl}"],
              (lambda c=c, w2=w2: lambda e: e.dma_start(w2[:, :], dr["sw2"][:, c * 4096:(c + 1) * 4096]))())
        it = 0
        for mh in range(4):
            for tc in range(4):
                pa_t, pa, poa = pool[(2 * it) % 6]
                pb_t, pb, pob = pool[(2 * it + 1) % 6]
                xtoks = [f"xnT{4 * tc + j}" for j in range(4)]
                for k in range(KT):
                    P.op("PE", xtoks + [f"sw1c{sl}"], [pa_t],
                         (lambda k=k, mh=mh, tc=tc, w1=w1, pa=pa, poa=poa: lambda e: e.matmul(
                             pa[:, poa:poa + 512],
                             w1[:, k * 512 + mh * 128: k * 512 + (mh + 1) * 128],
                             sb["xnT"][:, k * 2048 + tc * 512: k * 2048 + (tc + 1) * 512],
                             start=(k == 0), stop=(k == KT - 1)))())
                for k in range(KT):
                    P.op("PE", xtoks + [f"sw3c{sl}"], [pb_t],
                         (lambda k=k, mh=mh, tc=tc, w3=w3, pb=pb, pob=pob: lambda e: e.matmul(
                             pb[:, pob:pob + 512],
                             w3[:, k * 512 + mh * 128: k * 512 + (mh + 1) * 128],
                             sb["xnT"][:, k * 2048 + tc * 512: k * 2048 + (tc + 1) * 512],
                             start=(k == 0), stop=(k == KT - 1)))())
                ssl = "siluT0" if it % 2 == 0 else "siluT1"
                P.op("ACT", [pa_t], [ssl],
                     (lambda pa=pa, poa=poa, ssl=ssl: lambda e: e.activation(
                         sb[ssl][:, 0:512], pa[:, poa:poa + 512], AF.Silu))())
                P.op("DVE", [ssl, pb_t], [f"gT{mh}_{tc}"],
                     (lambda pb=pb, pob=pob, ssl=ssl, mh=mh, tc=tc: lambda e: e.tensor_tensor(
                         sb["gT"][:, mh * 2048 + tc * 512: mh * 2048 + (tc + 1) * 512],
                         sb[ssl][:, 0:512], pb[:, pob:pob + 512], ALU.mult))())
                it += 1
        for t in range(TT):
            for dc in range(2):
                pz_t, pz, poz = pool[it % 6]
                it += 1
                gtoks = [f"gT{mh}_{t // 4}" for mh in range(4)]
                for k in range(4):
                    P.op("PE", gtoks + [f"sw2c{sl}"], [pz_t],
                         (lambda k=k, t=t, dc=dc, w2=w2, pz=pz, poz=poz: lambda e: e.matmul(
                             pz[:, poz:poz + 512],
                             sb["gT"][:, k * 2048 + t * 128: k * 2048 + (t + 1) * 128],
                             w2[:, k * 1024 + dc * 512: k * 1024 + dc * 512 + 512],
                             start=(k == 0), stop=(k == 3)))())
                P.op("DVE", [pz_t, f"x{t}"], [f"x{t}"],
                     (lambda pz=pz, poz=poz, t=t, dc=dc: lambda e: e.scalar_tensor_tensor(
                         x[:, t * 1024 + dc * 512: t * 1024 + dc * 512 + 512],
                         pz[:, poz:poz + 512], 0.5,
                         x[:, t * 1024 + dc * 512: t * 1024 + dc * 512 + 512],
                         ALU.mult, ALU.add))())

    P.alias(SHARED_TOKS, ROUTED_TOKS)
    eslots = ["ew1", "ew2_0", "ew2_1"]
    for eidx in range(NEXP):
        t1 = eslots[(2 * eidx) % 3]
        t2 = eslots[(2 * eidx + 1) % 3]
        ew1s, ew2 = sb[t1], sb[t2]
        P.dma("POOL", [], [t1],
              (lambda eidx=eidx, ew1s=ew1s: lambda e: e.dma_start(
                  ew1s[:, :], dr["ew1"][:, eidx * 8192:(eidx + 1) * 8192]))())
        P.dma("POOL", [], [t2],
              (lambda eidx=eidx, ew2=ew2: lambda e: e.dma_start(
                  ew2[:, :], dr["ew2"][:, eidx * 8192:(eidx + 1) * 8192]))())
        for half in range(2):
            hsl = f"hT{(eidx * 2 + half) % 2}"
            hT = sb[hsl]
            xtoks = [f"xnT{half * 8 + j}" for j in range(8)]
            for mh in range(8):
                pp = P0 if mh % 2 == 0 else P1
                ptoks = ["P0a", "P0b"] if mh % 2 == 0 else ["P1a", "P1b"]
                for nc2 in range(2):
                    for k in range(KT):
                        P.op("PE", xtoks + [t1], [ptoks[nc2]],
                             (lambda k=k, mh=mh, half=half, nc2=nc2, pp=pp, ew1s=ew1s: lambda e: e.matmul(
                                 pp[:, nc2 * 512:(nc2 + 1) * 512],
                                 ew1s[:, k * 1024 + mh * 128: k * 1024 + (mh + 1) * 128],
                                 sb["xnT"][:, k * 2048 + half * 1024 + nc2 * 512:
                                           k * 2048 + half * 1024 + (nc2 + 1) * 512],
                                 start=(k == 0), stop=(k == KT - 1)))())
                P.op("ACT", ptoks, [f"{hsl}_{mh}"],
                     (lambda mh=mh, pp=pp, hT=hT: lambda e: e.activation(
                         hT[:, mh * 1024:(mh + 1) * 1024], pp[:, :], AF.Gelu))())
            ypool = [("P2a", P2, 0), ("P2b", P2, 512), ("P3", P3, 0)]
            it = 0
            for tl in range(8):
                t = half * 8 + tl
                for dc in range(2):
                    pz_t, pz, poz = ypool[it % 3]
                    it += 1
                    htoks = [f"{hsl}_{mh}" for mh in range(8)]
                    for k in range(KT):
                        P.op("PE", htoks + [t2], [pz_t],
                             (lambda k=k, tl=tl, dc=dc, pz=pz, poz=poz, hT=hT, ew2=ew2: lambda e: e.matmul(
                                 pz[:, poz:poz + 512],
                                 hT[:, k * 1024 + tl * 128: k * 1024 + (tl + 1) * 128],
                                 ew2[:, k * 1024 + dc * 512: k * 1024 + dc * 512 + 512],
                                 start=(k == 0), stop=(k == KT - 1)))())
                    P.op("DVE", [pz_t, f"x{t}", "comb05"], [f"x{t}"],
                         (lambda pz=pz, poz=poz, t=t, dc=dc, eidx=eidx: lambda e: e.scalar_tensor_tensor(
                             x[:, t * 1024 + dc * 512: t * 1024 + dc * 512 + 512],
                             pz[:, poz:poz + 512],
                             sb["comb05"][:, t * 8 + eidx: t * 8 + eidx + 1],
                             x[:, t * 1024 + dc * 512: t * 1024 + dc * 512 + 512],
                             ALU.mult, ALU.add))())



def attn_sublayer(P, dr, sb, psums):
    P0, P1, P2, P3, PTp = psums
    x, st = sb["x"], sb["stats"]
    KTb, V, wkv, wpj = sb["KTb"], sb["V"], sb["wkv"], sb["wpj"]
    xnTq, QTq, OTq, Ebuf, Bsb = sb["xnTq"], sb["QTq"], sb["OTq"], sb["Ebuf"], sb["Bsb"]

    ln_stats(P, sb, "ceps1", ("xnma",), psums)

    def xn_strips(tc):
        for j in range(4):
            t = tc * 4 + j
            xn_tile(P, sb, t, "xnma", sb["xnma"][:, :])
            dst = (lambda j=j: xnTq[:, :].rearrange("p (k s) -> p k s", s=512)[:, :, j * 128:(j + 1) * 128])
            transpose_tile(P, sb, psums, "xnma", sb["xnma"], "xnTq", dst)

    qpool = [("P2a", P2, 0), ("P2b", P2, 512), ("P3", P3, 0)]

    if "kv" not in ATTN_STAGES:
        return
    P.dma("POOL", ["wkv"], ["wkv"], lambda e: e.dma_start(wkv[:, :], dr["wk"][:, :]))
    it = 0
    for tc in range(4):
        xn_strips(tc)
        for mh in range(8):
            pz_t, pz, poz = qpool[it % 3]
            it += 1
            for k in range(KT):
                P.op("PE", ["xnTq", "wkv"], [pz_t],
                     (lambda k=k, mh=mh, pz=pz, poz=poz: lambda e: e.matmul(
                         pz[:, poz:poz + 512],
                         wkv[:, k * 1024 + mh * 128: k * 1024 + (mh + 1) * 128],
                         xnTq[:, k * 512:(k + 1) * 512],
                         start=(k == 0), stop=(k == KT - 1)))())
            P.op("DVE", [pz_t], [f"KTc{tc}"],
                 (lambda pz=pz, poz=poz, mh=mh, tc=tc: lambda e: e.tensor_copy(
                     KTb[:, mh * 2048 + tc * 512: mh * 2048 + (tc + 1) * 512],
                     pz[:, poz:poz + 512]))())

    P.dma("POOL", ["wkv"], ["wkv"], lambda e: e.dma_start(wkv[:, :], dr["wv"][:, :]))
    for t in range(TT):
        P.op("DVE", [], ["Vones"],
             (lambda t=t: lambda e: e.memset(
                 V[:, t * 1040: (t + 1) * 1040].rearrange("p (h c) -> p h c", c=65)[:, :, 64:65],
                 1.0))())
    it = 0
    for tc in range(4):
        xn_strips(tc)
        for j in range(4):
            t = tc * 4 + j
            for nc2 in range(2):
                pz_t, pz, poz = qpool[it % 3]
                it += 1
                for k in range(KT):
                    P.op("PE", ["xnTq", "wkv"], [pz_t],
                         (lambda k=k, j=j, nc2=nc2, pz=pz, poz=poz: lambda e: e.matmul(
                             pz[:, poz:poz + 512],
                             xnTq[:, k * 512 + j * 128: k * 512 + (j + 1) * 128],
                             wkv[:, k * 1024 + nc2 * 512: k * 1024 + nc2 * 512 + 512],
                             start=(k == 0), stop=(k == KT - 1)))())
                P.op("DVE", [pz_t, "Vones"], [f"Vt{t}"],
                     (lambda pz=pz, poz=poz, t=t, nc2=nc2: lambda e: e.tensor_copy(
                         V[:, t * 1040 + nc2 * 520: t * 1040 + nc2 * 520 + 520]
                         .rearrange("p (h c) -> p h c", c=65)[:, :, 0:64],
                         pz[:, poz:poz + 512].rearrange("p (h c) -> p h c", c=64)))())

    if "qt" not in ATTN_STAGES:
        return
    P.dma("POOL", ["wkv"], ["wkv"], lambda e: e.dma_start(wkv[:, :], dr["wq"][:, :]))
    P.dma("POOL", [], ["wpj"], lambda e: e.dma_start(wpj[:, :], dr["wproj"][:, :]))
    for qc in range(4):
        xn_strips(qc)
        for mh in range(8):
            for k in range(KT):
                P.op("PE", ["xnTq", "wkv"], ["P3"],
                     (lambda k=k, mh=mh: lambda e: e.matmul(
                         P3[:, :], wkv[:, k * 1024 + mh * 128: k * 1024 + (mh + 1) * 128],
                         xnTq[:, k * 512:(k + 1) * 512],
                         start=(k == 0), stop=(k == KT - 1)))())
            P.op("DVE", ["P3"], [f"QT{mh}"],
                 (lambda mh=mh: lambda e: e.tensor_copy(QTq[:, mh * 512:(mh + 1) * 512], P3[:, :]))())

        if "scores" not in ATTN_STAGES:
            continue
        ktoks = [f"KTc{c}" for c in range(4)]
        for pair in range(8):
            for k2 in range(8):
                for dk in range(2):
                    k = k2 * 2 + dk
                    for hh in range(2):
                        pp = P0 if hh == 0 else P1
                        pt = ["P0a", "P0b"] if hh == 0 else ["P1a", "P1b"]
                        lo = hh * 64
                        P.op("PE", ktoks + [f"QT{pair}"], [pt[dk]],
                             (lambda pair=pair, k=k, lo=lo, dk=dk, pp=pp: lambda e: e.matmul(
                                 pp[:, dk * 512:(dk + 1) * 512],
                                 KTb[lo:lo + 64, pair * 2048 + k * 128: pair * 2048 + (k + 1) * 128],
                                 QTq[lo:lo + 64, pair * 512:(pair + 1) * 512],
                                 start=True, stop=True))())
                for hh in range(2):
                    pp = P0 if hh == 0 else P1
                    pt = ["P0a", "P0b"] if hh == 0 else ["P1a", "P1b"]
                    P.op("ACT", pt, [f"E{hh}"],
                         (lambda pp=pp, hh=hh: lambda e: e.activation(
                             Ebuf[:, hh * 1024:(hh + 1) * 1024], pp[:, :], AF.Exp, scale=0.125))())
                warm(P, sb, psums)
                for hh in range(2):
                    h = pair * 2 + hh
                    po = P2
                    po_t = "P2a" if hh == 0 else "P2b"
                    poo = hh * 512
                    for dk in range(2):
                        k = k2 * 2 + dk
                        P.op("PE", [f"E{hh}", f"Vt{k}"], [po_t],
                             (lambda k=k, h=h, dk=dk, hh=hh, poo=poo: lambda e: e.matmul(
                                 P2[0:65, poo:poo + 512],
                                 V[:, k * 1040 + h * 65: k * 1040 + (h + 1) * 65],
                                 Ebuf[:, hh * 1024 + dk * 512: hh * 1024 + (dk + 1) * 512],
                                 start=(k == 0), stop=(k == 15)))())
            warm(P, sb, psums)
            for hh in range(2):
                h = pair * 2 + hh
                poo = hh * 512
                po_t = "P2a" if hh == 0 else "P2b"
                P.op("DVE", [po_t], ["rfull"],
                     (lambda poo=poo: lambda e: e.reciprocal(
                         sb["rfull"][64:65, :], P2[64:65, poo:poo + 512]))())
                P.op("PE", ["rfull", "onesb"], ["P3"],
                     lambda e: e.matmul(P3[0:64, :], sb["onesb"][64:65, 0:64],
                                        sb["rfull"][64:65, :], start=True, stop=True,
                                        tile_position=(64, 0)))
                P.op("DVE", ["P3"], ["Bsb"],
                     lambda e: e.tensor_copy(Bsb[0:64, :], P3[0:64, :]))
                P.op("DVE", [po_t, "Bsb"], [f"OT{h}"],
                     (lambda poo=poo, h=h: lambda e: e.tensor_tensor(
                         OTq[(h % 2) * 64:(h % 2) * 64 + 64, (h // 2) * 512:(h // 2 + 1) * 512],
                         P2[0:64, poo:poo + 512], Bsb[0:64, :], ALU.mult))())

        plevel = 4
        for sflag in ATTN_STAGES:
            if sflag.startswith("proj") and len(sflag) > 4:
                plevel = int(sflag[4:])
        if not any(s.startswith("proj") for s in ATTN_STAGES):
            continue
        otoks = [f"OT{h}" for h in range(16)]
        gsc = sb["gsc"]
        pA, tA = P0, ["P0a", "P0b"]
        pB, tB = P1, ["P1a", "P1b"]
        for j in range(4):
            t = qc * 4 + j
            for dc in range(2):
                for k in range(KT):
                    P.op("PE", otoks + ["wpj"], [tA[dc]],
                         (lambda k=k, j=j, dc=dc: lambda e: e.matmul(
                             pA[:, dc * 512:(dc + 1) * 512],
                             OTq[:, k * 512 + j * 128: k * 512 + (j + 1) * 128],
                             wpj[:, k * 1024 + dc * 512: k * 1024 + dc * 512 + 512],
                             start=(k == 0), stop=(k == KT - 1)))())
            if plevel == 1:
                P.op("DVE", tA + [f"x{t}"], [f"x{t}"],
                     (lambda t=t: lambda e: e.tensor_tensor(
                         x[:, t * 1024:(t + 1) * 1024], x[:, t * 1024:(t + 1) * 1024],
                         pA[:, :], ALU.add))())
                continue
            P.op("DVE", tA + [f"x{t}", "st_rstd"], tB,
                 (lambda t=t: lambda e: e.scalar_tensor_tensor(
                     pB[:, :], x[:, t * 1024:(t + 1) * 1024], st[:, 32 + t:33 + t],
                     pA[:, :], ALU.mult, ALU.add))())
            P.op("DVE", tB + ["st_mrs"], tA,
                 (lambda t=t: lambda e: e.tensor_scalar_sub(
                     pA[:, :], pB[:, :], st[:, 48 + t:49 + t]))())
            P.op("DVE", tA + ["bpj"], tB,
                 lambda e: e.tensor_tensor(pB[:, :], pA[:, :], sb["bpj"][:, :], ALU.add))
            if plevel == 2:
                P.op("DVE", tB + [f"x{t}"], [f"x{t}"],
                     (lambda t=t: lambda e: e.tensor_tensor(
                         x[:, t * 1024:(t + 1) * 1024], x[:, t * 1024:(t + 1) * 1024],
                         pB[:, :], ALU.add))())
                continue
            P.op("DVE", tB, ["l2s"],
                 lambda e: e.reduce_sum(gsc[:, 112:113], pB[:, :], axis=AX.X))
            if plevel == 35:
                P.op("DVE", [], ["l2q"],
                     lambda e: e.memset(gsc[:, 113:114], 1.0))
            elif plevel == 37:
                P.op("ACT", tB + ["PT"], ["l2q", "PT"],
                     lambda e: e.activation(
                         PTp[:, :], pB[:, :], AF.Square, accum_out=gsc[:, 113:114]))
            elif plevel == 41:
                P.op("ACT", tB, ["l2q", "xnma"],
                     lambda e: e.activation(
                         sb["xnma"][:, :], pB[:, :], AF.Exp, accum_out=gsc[:, 113:114]))
            elif plevel == 42:
                P.op("ACT", tB, ["xnma"],
                     lambda e: e.activation(sb["xnma"][:, :], pB[:, :], AF.Copy))
                P.op("ACT", ["xnma", "PT"], ["l2q", "PT"],
                     lambda e: e.activation(
                         PTp[:, :], sb["xnma"][:, :], AF.Square, accum_out=gsc[:, 113:114]))
            elif plevel == 39:
                P.op("ACT", tB, ["xnma"],
                     lambda e: e.activation(sb["xnma"][:, :], pB[:, :], AF.Square))
                P.op("DVE", ["xnma"], ["l2q"],
                     lambda e: e.reduce_sum(gsc[:, 113:114], sb["xnma"][:, :], axis=AX.X))
            else:
                P.op("ACT", tB, ["l2q", "xnma"],
                     lambda e: e.activation(
                         sb["xnma"][:, :], pB[:, :], AF.Square, accum_out=gsc[:, 113:114]))
            P.op("DVE", ["l2s"], ["l2m"],
                 lambda e: e.tensor_scalar_mul(gsc[:, 112:113], gsc[:, 112:113], 1.0 / 1024.0))
            P.op("DVE", ["l2q"], ["l2e"],
                 lambda e: e.tensor_scalar_mul(gsc[:, 113:114], gsc[:, 113:114], 1.0 / 1024.0))
            P.op("DVE", ["l2m"], ["l2ms"],
                 lambda e: e.tensor_mul(gsc[:, 114:115], gsc[:, 112:113], gsc[:, 112:113]))
            P.op("DVE", ["l2e", "l2ms"], ["l2v"],
                 lambda e: e.tensor_sub(gsc[:, 113:114], gsc[:, 113:114], gsc[:, 114:115]))
            if plevel == 36:
                P.op("DVE", ["l2v"], ["l2sd"],
                     lambda e: e.tensor_scalar_add(gsc[:, 114:115], gsc[:, 113:114], EPS2))
            else:
                P.op("ACT", ["l2v", "ceps2"], ["l2sd"],
                     lambda e: e.activation(gsc[:, 114:115], gsc[:, 113:114], AF.Sqrt,
                                            bias=sb["ceps2"][:, :]))
            P.op("DVE", ["l2sd"], ["l2r"],
                 lambda e: e.reciprocal(gsc[:, 114:115], gsc[:, 114:115]))
            P.op("DVE", ["l2r", "l2m"], ["l2mr"],
                 lambda e: e.tensor_mul(gsc[:, 115:116], gsc[:, 112:113], gsc[:, 114:115]))
            if plevel in (3, 35, 36, 37, 39, 41, 42):
                P.op("DVE", tB + [f"x{t}"], [f"x{t}"],
                     (lambda t=t: lambda e: e.tensor_tensor(
                         x[:, t * 1024:(t + 1) * 1024], x[:, t * 1024:(t + 1) * 1024],
                         pB[:, :], ALU.add))())
                continue
            P.op("DVE", tB + ["l2r", "l2mr"], tA,
                 lambda e: e.tensor_scalar(
                     pA[:, :], pB[:, :], gsc[:, 114:115], gsc[:, 115:116],
                     ALU.mult, ALU.subtract))
            P.op("DVE", tA + ["gam"], tB,
                 lambda e: e.tensor_tensor(pB[:, :], pA[:, :], sb["gam"][:, :], ALU.mult))
            P.op("DVE", tB + ["bet"], tA,
                 lambda e: e.tensor_tensor(pA[:, :], pB[:, :], sb["bet"][:, :], ALU.add))
            P.op("DVE", tA + [f"x{t}"], [f"x{t}"],
                 (lambda t=t: lambda e: e.tensor_tensor(
                     x[:, t * 1024:(t + 1) * 1024], x[:, t * 1024:(t + 1) * 1024],
                     pA[:, :], ALU.add))())



def _tile_rows(a, r=128):
    R, C = a.shape
    return np.ascontiguousarray(
        a.reshape(R // r, r, C).transpose(1, 0, 2).reshape(r, (R // r) * C))


def prep_inputs(L0_attn_x, L0_attn_w_q, L0_attn_w_k, L0_attn_w_v, L0_attn_w_proj,
                L0_attn_b_proj, L0_attn_ln_gamma, L0_attn_ln_beta,
                L0_ffn_router_w, L0_ffn_router_bias, L0_ffn_sw1, L0_ffn_sw3,
                L0_ffn_sw2, L0_ffn_ew1, L0_ffn_ew2, nb=NB, ncores=NCORES):
    xt = np.asarray(L0_attn_x, dtype=np.float32)

    def rep(v):
        v = np.asarray(v, np.float32)
        return np.ascontiguousarray(np.broadcast_to(v[None, :], (128, v.shape[0])))

    common = dict(
        wq=_tile_rows(np.asarray(L0_attn_w_q, np.float32)),
        wk=_tile_rows(np.asarray(L0_attn_w_k, np.float32)),
        wv=_tile_rows(np.asarray(L0_attn_w_v, np.float32)),
        wproj=_tile_rows(np.asarray(L0_attn_w_proj, np.float32)),
        sw1=np.ascontiguousarray(
            np.asarray(L0_ffn_sw1, np.float32).reshape(8, 128, 4, 512)
            .transpose(1, 2, 0, 3).reshape(128, 16384)),
        sw3=np.ascontiguousarray(
            np.asarray(L0_ffn_sw3, np.float32).reshape(8, 128, 4, 512)
            .transpose(1, 2, 0, 3).reshape(128, 16384)),
        sw2=np.ascontiguousarray(
            np.asarray(L0_ffn_sw2, np.float32).reshape(4, 4, 128, 1024)
            .transpose(2, 0, 1, 3).reshape(128, 16384)),
        ew1=np.ascontiguousarray(
            np.asarray(L0_ffn_ew1, np.float32).reshape(8, 8, 128, 1024)
            .transpose(2, 0, 1, 3).reshape(128, 8 * 8192)),
        ew2=np.ascontiguousarray(
            np.asarray(L0_ffn_ew2, np.float32).reshape(8, 8, 128, 1024)
            .transpose(2, 0, 1, 3).reshape(128, 8 * 8192)),
        rw=np.ascontiguousarray(
            np.asarray(L0_ffn_router_w, np.float32).T.reshape(8, 128, 8)
            .transpose(1, 0, 2).reshape(128, 64)),
        bpj=rep(L0_attn_b_proj), gam=rep(L0_attn_ln_gamma),
        bet=rep(L0_attn_ln_beta), rb=rep(L0_ffn_router_bias),
    )
    in_maps = []
    for c in range(ncores):
        xs = xt[c * nb:(c + 1) * nb]
        xs = np.ascontiguousarray(
            xs.reshape(nb, TT, 128, 1024).transpose(2, 0, 1, 3).reshape(128, nb * TT * 1024))
        m = dict(common)
        m["xin"] = xs
        in_maps.append(m)
    return in_maps


def kernel(**inputs):
    in_maps = prep_inputs(**inputs)
    nc = build_nc()
    res = run_bass_kernel_spmd(nc, in_maps, core_ids=list(range(NCORES)))
    outs = []
    for c in range(NCORES):
        yc = np.asarray(res.results[c]["y"])
        yc = yc.reshape(128, NB, TT, 1024).transpose(1, 2, 0, 3).reshape(NB, S, D)
        outs.append(yc)
    return np.concatenate(outs, axis=0).astype(np.float32)
